# version 01
# baseline (speedup 1.0000x reference)
import contextlib
import numpy as np
import ml_dtypes
import concourse.bass as bass
import concourse.mybir as mybir
from concourse.bass_utils import run_bass_kernel_spmd

F32 = mybir.dt.float32
BF16 = mybir.dt.bfloat16
AF = mybir.ActivationFunctionType
ALU = mybir.AluOpType
BF = ml_dtypes.bfloat16

ENGS = ("tensor", "vector", "scalar", "gpsimd", "sync")
D = 2048
NT = 2048
MQKV = 6144
HF = 8192
INV_SQRT_HD = float(1.0 / np.sqrt(128.0))
FWHT_SCALE = float(1.0 / np.sqrt(8192.0))
LN_EPS = 1e-5


class Sem:
    __slots__ = ("h", "n")

    def __init__(self, h):
        self.h = h
        self.n = 0

    def mark(self):
        return (self, self.n)


class Tape:

    def __init__(self, nc, ctx):
        self.nc = nc
        self.tape = {e: [] for e in ENGS}
        self._ctx = ctx
        self._sems = {}
        self.wr = {}
        self.rd = {}

    def sem(self, name):
        if name not in self._sems:
            self._sems[name] = Sem(self._ctx.enter_context(self.nc.semaphore(name)))
        return self._sems[name]

    def op(self, eng, fn, wait=(), inc=(), dma=False, reads=(), writes=()):
        waits = []
        for w in wait:
            if isinstance(w, Sem):
                waits.append((w, w.n))
            else:
                waits.append((w[0], w[1]))
        for r in reads:
            m = self.wr.get(r)
            if m is not None:
                waits.append(m)
        for r in writes:
            m = self.wr.get(r)
            if m is not None:
                waits.append(m)
            for mm in self.rd.get(r, ()):
                waits.append(mm)
        byname = {}
        for s, v in waits:
            if v > 0 and (s not in byname or v > byname[s]):
                byname[s] = v
        waits = [(s, v) for s, v in byname.items()]
        incs = []
        for i in inc:
            if isinstance(i, Sem):
                incs.append((i, 16 if dma else 1))
            else:
                incs.append((i[0], i[1]))
        for s, k in incs:
            s.n += k
        self.tape[eng].append((waits, fn, incs))
        mark = (incs[-1][0], incs[-1][0].n) if incs else None
        if mark is not None:
            for r in writes:
                self.wr[r] = mark
                self.rd[r] = []
            for r in reads:
                self.rd.setdefault(r, []).append(mark)

    def note_read(self, r, mark):
        self.rd.setdefault(r, []).append(mark)

    def note_write(self, r, mark):
        self.wr[r] = mark
        self.rd[r] = []

    def raw_wait(self, eng, sem, val):
        self.tape[eng].append(([(sem, val)], None, []))

    def barrier(self, name):
        b = self.sem("bar_" + name)
        target = b.n + 5
        for e in ENGS:
            self.tape[e].append(([], lambda h: h.nop(), [(b, 1)]))
        b.n = target
        for e in ENGS:
            self.tape[e].append(([(b, target)], None, []))
        self.wr = {k: v for k, v in self.wr.items() if k.startswith("dma:")}
        self.rd = {}

    def replay(self, block):
        def mk(eng_name):
            def body(h):
                for waits, fn, incs in self.tape[eng_name]:
                    for s, v in waits:
                        h.wait_ge(s.h, v)
                    if fn is None:
                        continue
                    instr = fn(h)
                    for s, k in incs:
                        instr.then_inc(s.h, k)
            return body

        block.tensor(mk("tensor"))
        block.vector(mk("vector"))
        block.scalar(mk("scalar"))
        block.gpsimd(mk("gpsimd"))
        block.sync(mk("sync"))


def build_program():
    nc = bass.Bass("TRN2", target_bir_lowering=False)

    xa = nc.dram_tensor("xa", [NT, D], F32, kind="ExternalInput").ap()
    xb = nc.dram_tensor("xb", [NT, D], F32, kind="ExternalInput").ap()
    xbt = nc.dram_tensor("xbt", [D, NT], F32, kind="ExternalInput").ap()
    wqkvT = nc.dram_tensor("wqkvT", [D, MQKV], F32, kind="ExternalInput").ap()
    wprojT = nc.dram_tensor("wprojT", [D, D], F32, kind="ExternalInput").ap()
    w1T = nc.dram_tensor("w1T", [D, HF], F32, kind="ExternalInput").ap()
    w3T = nc.dram_tensor("w3T", [D, HF], F32, kind="ExternalInput").ap()
    w2T = nc.dram_tensor("w2T", [HF, D], F32, kind="ExternalInput").ap()
    c_id = nc.dram_tensor("c_id", [128, 128], BF16, kind="ExternalInput").ap()
    c_h = nc.dram_tensor("c_h", [128, 128], BF16, kind="ExternalInput").ap()
    c_m = nc.dram_tensor("c_m", [128, 128], BF16, kind="ExternalInput").ap()
    yT = nc.dram_tensor("yT", [D, NT], F32, kind="ExternalOutput").ap()
    s_v = nc.dram_tensor("s_v", [NT, D], BF16, kind="Internal").ap()
    s_hid = nc.dram_tensor("s_hid", [HF, NT], BF16, kind="Internal").ap()
    s_w2h = nc.dram_tensor("s_w2h", [HF, D], BF16, kind="Internal").ap()

    ctx = contextlib.ExitStack()
    sb = lambda name, shape, dt: ctx.enter_context(nc.sbuf_tensor(name, shape, dt))

    A1 = sb("A1", [128, 32768], BF16)
    A2 = sb("A2", [128, 32768], BF16)
    WB = sb("WB", [128, 8192], BF16)
    XF = sb("XF", [128, 8192], BF16)
    XC = sb("XC", [128, 4096], BF16)
    VINT = sb("VINT", [128, 4128], BF16)
    VST = sb("VST", [128, 512], BF16)
    SAB = sb("SAB", [128, 1024], BF16)
    SBB = sb("SBB", [128, 1024], BF16)
    GBUF = sb("GBUF", [128, 1024], BF16)
    HST = sb("HST", [128, 1024], BF16)
    YST = sb("YST", [128, 1024], F32)
    XBT = sb("XBT", [128, 1024], F32)
    EBUF = sb("EBUF", [128, 256], BF16)
    OBUF = sb("OBUF", [128, 256], BF16)
    RC = sb("RC", [128, 2], F32)
    DUMP = sb("DUMP", [128, 2048], BF16)
    SUMS = sb("SUMS", [128, 16], F32)
    SQS = sb("SQS", [128, 16], F32)
    MNEG = sb("MNEG", [128, 16], F32)
    RSTD = sb("RSTD", [128, 16], F32)
    TMPA = sb("TMPA", [128, 16], F32)
    TMPB = sb("TMPB", [128, 16], F32)
    CID = sb("CID", [128, 128], BF16)
    CH = sb("CH", [128, 128], BF16)
    CM = sb("CM", [128, 128], BF16)

    pb = [ctx.enter_context(nc.psum_tensor(f"pb{i}", [128, 512], F32)) for i in range(8)]
    pbf = [p[:, :].bitcast(BF16) for p in pb]
    XFf32 = XF[:, :].bitcast(F32)

    tp = Tape(nc, ctx)
    S = tp.sem
    CLD = S("CLD")

    for dst, src in ((CID, c_id), (CH, c_h), (CM, c_m)):
        tp.op("sync", lambda h, d=dst, s=src: h.dma_start(d[:, :], s), inc=[CLD], dma=True)
    cld = CLD.mark()
    ones_ap = bass.AP(VINT, 128, [[4128, 128], [129, 32], [1, 1]])
    tp.op("vector", lambda h: h.memset(ones_ap, 1.0), inc=[S("MISC")], writes=["vint_ones"])

    def emit_ln_tile(xsrc, tile, xn_col_base, P):
        b = tile % 2
        xld = XFf32[:, b * 2048:(b + 1) * 2048]
        xcb = XC[:, b * 2048:b * 2048 + 2048]
        col = tile % 16
        XL, XR, ST, VS, TT, TE = (S(P + "XL"), S(P + "XR"), S(P + "ST"),
                                  S(P + "VS"), S(P + "TT"), S(P + "TE"))
        tp.op("sync", lambda h, x=xld, t=tile: h.dma_start(
            x, xsrc[128 * t:128 * (t + 1), :]), inc=[XL], dma=True,
            writes=[f"xf{b}"])
        tp.op("scalar", lambda h, x=xld, c=col: h.activation(
            DUMP[:, :], x, AF.Copy, bias=0.0, scale=np.float32(1.0),
            accum_out=SUMS[:, c:c + 1]), inc=[ST],
            reads=[f"xf{b}"], writes=[f"sum{col}"])
        tp.op("scalar", lambda h, x=xld, c=col: h.activation(
            DUMP[:, :], x, AF.Square, accum_out=SQS[:, c:c + 1]), inc=[ST],
            reads=[f"xf{b}"], writes=[f"sq{col}"])
        tp.op("vector", lambda h, c=col: h.tensor_scalar(
            MNEG[:, c:c + 1], SUMS[:, c:c + 1], float(-1.0 / 2048.0), None,
            ALU.mult), inc=[VS], reads=[f"sum{col}"], writes=[f"mneg{col}"])
        tp.op("vector", lambda h, c=col: h.tensor_scalar(
            TMPA[:, c:c + 1], SQS[:, c:c + 1], float(1.0 / 2048.0), None,
            ALU.mult), inc=[VS], reads=[f"sq{col}"], writes=[f"ta{col}"])
        tp.op("vector", lambda h, c=col: h.tensor_mul(
            TMPB[:, c:c + 1], MNEG[:, c:c + 1], MNEG[:, c:c + 1]), inc=[VS],
            reads=[f"mneg{col}"], writes=[f"tb{col}"])
        tp.op("vector", lambda h, c=col: h.tensor_sub(
            TMPA[:, c:c + 1], TMPA[:, c:c + 1], TMPB[:, c:c + 1]), inc=[VS],
            reads=[f"tb{col}"], writes=[f"ta{col}"])
        tp.op("scalar", lambda h, c=col: h.activation(
            TMPB[:, c:c + 1], TMPA[:, c:c + 1], AF.Ln, bias=np.float32(LN_EPS)),
            inc=[ST], reads=[f"ta{col}"], writes=[f"tb{col}"])
        tp.op("scalar", lambda h, c=col: h.activation(
            RSTD[:, c:c + 1], TMPB[:, c:c + 1], AF.Exp, scale=np.float32(-0.5)),
            inc=[ST], reads=[f"tb{col}"], writes=[f"rstd{col}"])
        tp.op("vector", lambda h, x=xld, xc=xcb, c=col: h.tensor_scalar(
            xc, x, MNEG[:, c:c + 1], RSTD[:, c:c + 1], ALU.add, ALU.mult),
            inc=[XR], reads=[f"xf{b}", f"mneg{col}", f"rstd{col}"],
            writes=[f"xc{b}"])
        for dt in range(16):
            bank = 6 + dt % 2
            tp.op("tensor", lambda h, xc=xcb, d=dt, bk=bank: h.transpose(
                pbf[bk][:, 0:128], xc[:, d * 128:(d + 1) * 128], CID[:, :]),
                wait=[cld], inc=[TT], reads=[f"xc{b}"], writes=[f"pb{bank}"])
            dcol = xn_col_base(dt, tile)
            tp.op("vector", lambda h, bk=bank, c=dcol: h.tensor_copy(
                A2[:, c:c + 128], pbf[bk][:, 0:128]), inc=[TE],
                reads=[f"pb{bank}"], writes=["A2lo" if P == "A" else "A2"])
        return TE.mark()

    WLD, VMM, VEV, SVW = S("WLD"), S("VMM"), S("VEV"), S("SVW")
    QKM, QEV, VGL = S("QKM"), S("QEV"), S("VGL")
    GQK, GEX, GMS, GAV, GRC, GTS, GTP, GPT = (
        S("GQK"), S("GEX"), S("GMS"), S("GAV"), S("GRC"), S("GTS"), S("GTP"), S("GPT"))

    for half in range(2):
        for t8 in range(8):
            emit_ln_tile(xa, half * 8 + t8,
                         lambda dt, t: dt * 1024 + (t % 8) * 128, "A")
        for mc in range(8):
            wbb = mc % 2
            src = wqkvT.rearrange("(dt p) m -> p dt m", p=128)[
                :, :, 4096 + mc * 256: 4096 + (mc + 1) * 256]
            dst = WB[:, wbb * 4096:(wbb + 1) * 4096].rearrange("p (dt m) -> p dt m", dt=16)
            tp.op("gpsimd", lambda h, d=dst, s=src: h.dma_start(d, s),
                  inc=[WLD], dma=True, writes=[f"wb{wbb}"])
            for tt in range(8):
                bank = (mc * 8 + tt) % 2
                for dt in range(16):
                    tp.op("tensor", lambda h, b=bank, d=dt, t=tt, w=wbb: h.matmul(
                        pb[b][:, 0:256],
                        A2[:, d * 1024 + t * 128: d * 1024 + (t + 1) * 128],
                        WB[:, w * 4096 + d * 256: w * 4096 + d * 256 + 256],
                        start=(d == 0), stop=(d == 15)),
                        inc=([VMM] if dt == 15 else []),
                        reads=(["A2lo", f"wb{wbb}"] if dt == 0 else []),
                        writes=([f"pb{bank}"] if dt == 0 else []))
                vm = VMM.mark()
                tp.note_write(f"pb{bank}", vm)
                tp.note_read("A2lo", vm)
                tp.note_read(f"wb{wbb}", vm)
                tp.op("vector", lambda h, b=bank: h.tensor_copy(
                    VST[:, b * 256:(b + 1) * 256], pb[b][:, 0:256]),
                    inc=[VEV], reads=[f"pb{bank}"], writes=[f"vst{bank}"])
                tp.op("sync", lambda h, b=bank, t=tt, m=mc, hf=half: h.dma_start(
                    s_v[128 * (hf * 8 + t):128 * (hf * 8 + t + 1),
                        m * 256:(m + 1) * 256],
                    VST[:, b * 256:(b + 1) * 256]),
                    inc=[SVW], dma=True, reads=[f"vst{bank}"])
        v_done = SVW.mark()
        for qr in range(2):
            qq = half * 2 + qr
            for mc in range(16):
                wbb = mc % 2
                src = wqkvT.rearrange("(dt p) m -> p dt m", p=128)[
                    :, :, mc * 256:(mc + 1) * 256]
                dst = WB[:, wbb * 4096:(wbb + 1) * 4096].rearrange(
                    "p (dt m) -> p dt m", dt=16)
                tp.op("gpsimd", lambda h, d=dst, s=src: h.dma_start(d, s),
                      inc=[WLD], dma=True, writes=[f"wb{wbb}"])
                for mt2 in range(2):
                    mt = 2 * mc + mt2
                    bank = mt % 4
                    for dt in range(16):
                        tp.op("tensor", lambda h, b=bank, d=dt, q=qr, w=wbb, m2=mt2: h.matmul(
                            pb[b][:, 0:512],
                            WB[:, w * 4096 + d * 256 + m2 * 128:
                               w * 4096 + d * 256 + m2 * 128 + 128],
                            A2[:, d * 1024 + q * 512: d * 1024 + (q + 1) * 512],
                            start=(d == 0), stop=(d == 15)),
                            inc=([QKM] if dt == 15 else []),
                            reads=(["A2lo", f"wb{wbb}"] if dt == 0 else []),
                            writes=([f"pb{bank}"] if dt == 0 else []))
                    qm = QKM.mark()
                    tp.note_write(f"pb{bank}", qm)
                    tp.note_read("A2lo", qm)
                    tp.note_read(f"wb{wbb}", qm)
                    i = mt % 16
                    base = 16384 + (8192 if mt >= 16 else 0) + i
                    dstq = bass.AP(A2, base, [[32768, 128], [128, 64], [16, 8]])
                    srcq = bass.AP(pb[bank], 0, [[512, 128], [8, 64], [1, 8]])
                    tp.op("vector", lambda h, d=dstq, s=srcq: h.tensor_copy(d, s),
                          inc=[QEV], reads=[f"pb{bank}"],
                          writes=["rQI" if mt < 16 else "rKI"])
            qk_done = QEV.mark()
            for tt4 in range(4):
                TTI = 4 * qq + tt4
                vb = tt4 % 2
                for tau in range(8):
                    dstv = bass.AP(VINT, tau * 16 * 4128 + vb * 2064,
                                   [[4128, 16], [129, 16], [1, 128]])
                    srcv = bass.AP(s_v.tensor, (128 * TTI + tau) * 2048,
                                   [[128, 16], [8 * 2048, 16], [1, 128]])
                    tp.op("sync", lambda h, d=dstv, s=srcv: h.dma_start(d, s),
                          wait=([v_done] if tau == 0 else []), inc=[VGL], dma=True,
                          reads=(["vint_ones"] if tau == 0 else []),
                          writes=([f"vint{vb}"] if tau == 0 else []),
                          )
                vload = VGL.mark()
                tp.note_write(f"vint{vb}", vload)
                for g16 in range(16):
                    gq = tt4 * 16 + g16
                    Gg = 64 * qq + gq
                    p2 = g16 % 2
                    tp.op("tensor", lambda h, g=gq, b=p2: h.matmul(
                        pb[b][:, 0:128],
                        A2[:, 24576 + g * 128: 24576 + (g + 1) * 128],
                        A2[:, 16384 + g * 128: 16384 + (g + 1) * 128],
                        start=True, stop=True),
                        wait=[qk_done], inc=[GQK],
                        reads=["rQI", "rKI"], writes=[f"pb{p2}"])
                    tp.op("scalar", lambda h, b=p2: h.activation(
                        EBUF[:, b * 128:(b + 1) * 128], pb[b][:, 0:128], AF.Exp,
                        bias=np.float32(-8.0), scale=np.float32(INV_SQRT_HD)),
                        inc=[GEX], reads=[f"pb{p2}"], writes=[f"eb{p2}"])
                    tp.op("vector", lambda h, b=p2: h.tensor_mul(
                        EBUF[:, b * 128:(b + 1) * 128],
                        EBUF[:, b * 128:(b + 1) * 128], CM[:, :]),
                        wait=[cld], inc=[GMS], reads=[f"eb{p2}"], writes=[f"eb{p2}"])
                    tp.op("tensor", lambda h, b=p2, g=g16, v=vb: h.matmul(
                        pb[2 + b][:, 0:129],
                        EBUF[:, b * 128:(b + 1) * 128],
                        VINT[:, v * 2064 + g * 129: v * 2064 + (g + 1) * 129],
                        start=True, stop=True),
                        wait=[vload], inc=[GAV],
                        reads=[f"eb{p2}", f"vint{vb}"], writes=[f"pb{2 + p2}"])
                    tp.op("vector", lambda h, b=p2: h.reciprocal(
                        RC[:, b:b + 1], pb[2 + b][:, 128:129]),
                        inc=[GRC], reads=[f"pb{2 + p2}"], writes=[f"rc{p2}"])
                    tp.op("vector", lambda h, b=p2: h.tensor_scalar(
                        OBUF[:, b * 128:(b + 1) * 128], pb[2 + b][:, 0:128],
                        RC[:, b:b + 1], None, ALU.mult),
                        inc=[GTS], reads=[f"pb{2 + p2}", f"rc{p2}"],
                        writes=[f"ob{p2}"])
                    tp.op("tensor", lambda h, b=p2: h.transpose(
                        pbf[4 + b][:, 0:128], OBUF[:, b * 128:(b + 1) * 128],
                        CID[:, :]),
                        inc=[GTP], reads=[f"ob{p2}"], writes=[f"pb{4 + p2}"])
                    gg = Gg // 2
                    ctb = 8 * (Gg % 2)
                    dstp = bass.AP(A1, ctb * 2048 + gg,
                                   [[32768, 128], [2048, 8], [128, 16]])
                    srcp = pbf[4 + p2][:, 0:128].rearrange("p (t i) -> p t i", t=8)
                    tp.op("vector", lambda h, d=dstp, s=srcp: h.tensor_copy(d, s),
                          inc=[GPT], reads=[f"pb{4 + p2}"])

    tp.barrier("AB")

    for tile in range(16):
        emit_ln_tile(xb, tile, lambda dt, t: dt * 2048 + t * 128, "B")
    tp.barrier("B0")

    BMA, BMB, SIL, ULC, GTM, ZMM, ZEV, HSW = (
        S("BMA"), S("BMB"), S("SIL"), S("ULC"), S("GTM"), S("ZMM"), S("ZEV"),
        S("HSW"))
    BLD, BFY, BSW = S("BLD"), S("BFY"), S("BSW")

    def emit_butterfly_chunk(bc):
        src = w2T.rearrange("(C p) d -> p C d", p=128)[:, :, bc * 64:(bc + 1) * 64]
        dstA = XF[:, 0:4096].rearrange("p (C d) -> p C d", C=64)
        tp.op("gpsimd", lambda h, d=dstA, s=src: h.dma_start(d, s),
              inc=[BLD], dma=True, writes=["bfA"], reads=["dma:bsw"])
        for st in range(6):
            s2 = 1 << st
            sof = (st % 2) * 4096
            dof = ((st + 1) % 2) * 4096
            na = 32 // s2
            inner = s2 * 64
            rs = ["bfA" if st % 2 == 0 else "bfB"]
            ws = ["bfB" if st % 2 == 0 else "bfA"]
            a_lo = bass.AP(XF, sof, [[8192, 128], [2 * inner, na], [1, inner]])
            a_hi = bass.AP(XF, sof + inner, [[8192, 128], [2 * inner, na], [1, inner]])
            o_lo = bass.AP(XF, dof, [[8192, 128], [2 * inner, na], [1, inner]])
            o_hi = bass.AP(XF, dof + inner, [[8192, 128], [2 * inner, na], [1, inner]])
            tp.op("vector", lambda h, o=o_lo, a=a_lo, b=a_hi: h.tensor_add(o, a, b),
                  inc=[BFY], reads=rs, writes=[])
            tp.op("vector", lambda h, o=o_hi, a=a_lo, b=a_hi: h.tensor_sub(o, a, b),
                  inc=[BFY], reads=rs, writes=ws)
        dsts = s_w2h.rearrange("(C p) d -> p C d", p=128)[:, :, bc * 64:(bc + 1) * 64]
        srcs = XF[:, 0:4096].rearrange("p (C d) -> p C d", C=64)
        tp.op("sync", lambda h, d=dsts, s=srcs: h.dma_start(d, s),
              inc=[BSW], dma=True, reads=["bfA"], writes=["dma:bsw"])

    for ht in range(64):
        hb = ht % 2
        for widx, wt in enumerate((w1T, w3T)):
            src = wt.rearrange("(dt p) m -> p dt m", p=128)[
                :, :, ht * 128:(ht + 1) * 128]
            qoff = hb * 4096 + widx * 2048
            dst = WB[:, qoff:qoff + 2048].rearrange("p (dt m) -> p dt m", dt=16)
            tp.op("gpsimd", lambda h, d=dst, s=src: h.dma_start(d, s),
                  inc=[WLD], dma=True, writes=[f"wq{hb}{widx}"])
        for rb in range(4):
            bA = rb % 2
            bB = 2 + rb % 2
            for widx, (bk, msem) in enumerate(((bA, BMA), (bB, BMB))):
                qoff = hb * 4096 + widx * 2048
                for dt in range(16):
                    tp.op("tensor", lambda h, b=bk, d=dt, qo=qoff, r=rb: h.matmul(
                        pb[b][:, 0:512],
                        WB[:, qo + d * 128: qo + (d + 1) * 128],
                        A2[:, d * 2048 + r * 512: d * 2048 + (r + 1) * 512],
                        start=(d == 0), stop=(d == 15)),
                        inc=([msem] if dt == 15 else []),
                        reads=(["A2", f"wq{hb}{widx}"] if dt == 0 else []),
                        writes=([f"pb{bk}"] if dt == 0 else []))
                bm = msem.mark()
                tp.note_write(f"pb{bk}", bm)
                tp.note_read("A2", bm)
                tp.note_read(f"wq{hb}{widx}", bm)
            tp.op("scalar", lambda h, b=bA, r=rb: h.activation(
                SAB[:, (r % 2) * 512:(r % 2) * 512 + 512], pb[b][:, 0:512],
                AF.Silu), inc=[SIL], reads=[f"pb{bA}"], writes=[f"sab{rb % 2}"])
            tp.op("scalar", lambda h, b=bB, r=rb: h.activation(
                SBB[:, (r % 2) * 512:(r % 2) * 512 + 512], pb[b][:, 0:512],
                AF.Copy, bias=0.0, scale=np.float32(FWHT_SCALE)),
                inc=[ULC], reads=[f"pb{bB}"], writes=[f"sbb{rb % 2}"])
            tp.op("vector", lambda h, r=rb: h.tensor_mul(
                GBUF[:, (r % 2) * 512:(r % 2) * 512 + 512],
                SAB[:, (r % 2) * 512:(r % 2) * 512 + 512],
                SBB[:, (r % 2) * 512:(r % 2) * 512 + 512]),
                inc=[GTM], reads=[f"sab{rb % 2}", f"sbb{rb % 2}"],
                writes=[f"gb{rb % 2}"])
            zb = 4 + rb % 2
            tp.op("tensor", lambda h, r=rb, z=zb: h.matmul(
                pb[z][:, 0:512], CH[:, :],
                GBUF[:, (r % 2) * 512:(r % 2) * 512 + 512],
                start=True, stop=True), wait=[cld],
                inc=[ZMM], reads=[f"gb{rb % 2}"], writes=[f"pb{zb}"])
            tp.op("vector", lambda h, r=rb, z=zb: h.tensor_copy(
                HST[:, (r % 2) * 512:(r % 2) * 512 + 512], pb[z][:, 0:512]),
                inc=[ZEV], reads=[f"pb{zb}"], writes=[f"hst{rb % 2}"])
            tp.op("sync", lambda h, r=rb, t=ht: h.dma_start(
                s_hid[128 * t:128 * (t + 1), r * 512:(r + 1) * 512],
                HST[:, (r % 2) * 512:(r % 2) * 512 + 512]),
                inc=[HSW], dma=True, reads=[f"hst{rb % 2}"], writes=["dma:hsw"])
        if ht % 2 == 1:
            emit_butterfly_chunk(ht // 2)

    tp.barrier("BC")
    hid_done = HSW.mark()
    w2h_done = BSW.mark()

    HQL, WCL, PJL, XBL, CMM, YEV, YSW = (
        S("HQL"), S("WCL"), S("PJL"), S("XBL"), S("CMM"), S("YEV"), S("YSW"))
    for q in range(4):
        srch = s_hid.rearrange("(ht p) r -> p ht r", p=128)[
            :, :, q * 512:(q + 1) * 512]
        dsth = A2[:, :].rearrange("p (ht r) -> p ht r", ht=64)
        tp.op("sync", lambda h, d=dsth, s=srch: h.dma_start(d, s),
              wait=[hid_done], inc=[HQL], dma=True, writes=["A2"])
        for dct in range(16):
            wb2 = dct % 2
            wcol = WB[:, :] if wb2 == 0 else XF[:, :]
            wcol_res = "wb_c" if wb2 == 0 else "xf_c"
            srcw = s_w2h.rearrange("(C p) d -> p C d", p=128)[
                :, :, dct * 128:(dct + 1) * 128]
            tp.op("sync", lambda h, d=wcol.rearrange("p (C d) -> p C d", C=64),
                  s=srcw: h.dma_start(d, s),
                  wait=[w2h_done], inc=[WCL], dma=True, writes=[wcol_res])
            srcp = wprojT.rearrange("(ct p) d -> p ct d", p=128)[
                :, :, dct * 128:(dct + 1) * 128]
            dstp = XC[:, wb2 * 2048:(wb2 + 1) * 2048].rearrange(
                "p (ct d) -> p ct d", ct=16)
            tp.op("gpsimd", lambda h, d=dstp, s=srcp: h.dma_start(d, s),
                  inc=[PJL], dma=True, writes=[f"xc{wb2}"])
            tp.op("sync", lambda h, d=dct, qi=q, w=wb2: h.dma_start(
                XBT[:, w * 512:(w + 1) * 512],
                xbt[128 * d:128 * (d + 1), qi * 512:(qi + 1) * 512]),
                inc=[XBL], dma=True, writes=[f"xbt{wb2}"])
            bko = dct % 2
            for ct in range(16):
                tp.op("tensor", lambda h, c=ct, w=wb2, qi=q, b=bko: h.matmul(
                    pb[b][:, 0:512],
                    XC[:, w * 2048 + c * 128: w * 2048 + (c + 1) * 128],
                    A1[:, c * 2048 + qi * 512: c * 2048 + qi * 512 + 512],
                    start=(c == 0), stop=False),
                    reads=([f"xc{wb2}", wcol_res, "A2"] if ct == 0 else []),
                    writes=([f"pb{bko}"] if ct == 0 else []))
            for C in range(64):
                tp.op("tensor", lambda h, c=C, w=wcol, b=bko: h.matmul(
                    pb[b][:, 0:512],
                    w[:, c * 128:(c + 1) * 128],
                    A2[:, c * 512:(c + 1) * 512],
                    start=False, stop=(c == 63)),
                    inc=([CMM] if C == 63 else []))
            cm2 = CMM.mark()
            tp.note_write(f"pb{bko}", cm2)
            tp.note_read(f"xc{wb2}", cm2)
            tp.note_read(wcol_res, cm2)
            tp.note_read("A2", cm2)
            tp.op("vector", lambda h, b=bko, w=wb2: h.tensor_add(
                YST[:, w * 512:(w + 1) * 512], pb[b][:, 0:512],
                XBT[:, w * 512:(w + 1) * 512]),
                inc=[YEV], reads=[f"pb{bko}", f"xbt{wb2}"], writes=[f"yst{wb2}"])
            tp.op("sync", lambda h, d=dct, qi=q, w=wb2: h.dma_start(
                yT[128 * d:128 * (d + 1), qi * 512:(qi + 1) * 512],
                YST[:, w * 512:(w + 1) * 512]),
                inc=[YSW], dma=True, reads=[f"yst{wb2}"])
    tp.raw_wait("sync", YSW, YSW.n)

    with nc.Block() as block:
        tp.replay(block)
    ctx.close()
    return nc


_NC_CACHE = {}


def _get_program():
    if "nc" not in _NC_CACHE:
        _NC_CACHE["nc"] = build_program()
    return _NC_CACHE["nc"]


def _consts():
    H = np.array([[1.0]], np.float32)
    while H.shape[0] < 128:
        H = np.block([[H, H], [H, -H]])
    cid = np.eye(128, dtype=BF)
    cm = np.zeros((128, 128), dtype=BF)
    for t in range(8):
        cm[16 * t:16 * t + 16, 16 * t:16 * t + 16] = 1.0
    return cid, H.astype(BF), cm


def kernel(L0_attn_x, L0_attn_w_qkv, L0_attn_w_proj, L0_ffn_W1, L0_ffn_W2,
           L0_ffn_W3):
    X = np.ascontiguousarray(np.asarray(L0_attn_x, dtype=np.float32)[0])
    wqkvT = np.ascontiguousarray(np.asarray(L0_attn_w_qkv, np.float32).T)
    wprojT = np.ascontiguousarray(np.asarray(L0_attn_w_proj, np.float32).T)
    w1T = np.ascontiguousarray(np.asarray(L0_ffn_W1, np.float32).T)
    w3T = np.ascontiguousarray(np.asarray(L0_ffn_W3, np.float32).T)
    w2T = np.ascontiguousarray(np.asarray(L0_ffn_W2, np.float32).T)
    cid, ch, cm = _consts()

    in_maps = []
    s_list = []
    for c in range(8):
        Sidx = np.array([h * 1024 + 128 * c + gg for h in range(16)
                         for gg in range(128)])
        s_list.append(Sidx)
        xbv = np.ascontiguousarray(X[Sidx])
        in_maps.append(dict(
            xa=np.ascontiguousarray(X[2048 * c:2048 * (c + 1)]),
            xb=xbv,
            xbt=np.ascontiguousarray(xbv.T),
            wqkvT=wqkvT, wprojT=wprojT, w1T=w1T, w3T=w3T, w2T=w2T,
            c_id=cid, c_h=ch, c_m=cm,
        ))

    nc = _get_program()
    res = run_bass_kernel_spmd(nc, in_maps, core_ids=list(range(8)))
    out = np.empty((16384, 2048), np.float32)
    for c in range(8):
        out[s_list[c]] = res.results[c]["yT"].T
    return out.reshape(1, 16384, 2048)


# version 04
# speedup vs baseline: 1.2680x; 1.2680x over previous
import contextlib
import numpy as np
import ml_dtypes
import concourse.bass as bass
import concourse.mybir as mybir
from concourse.bass_utils import run_bass_kernel_spmd

F32 = mybir.dt.float32
BF16 = mybir.dt.bfloat16
AF = mybir.ActivationFunctionType
ALU = mybir.AluOpType
BF = ml_dtypes.bfloat16

ENGS = ("tensor", "vector", "scalar", "gpsimd", "sync")
D = 2048
NT = 2048
MQKV = 6144
HF = 8192
INV_SQRT_HD = float(1.0 / np.sqrt(128.0))
FWHT_SCALE = float(1.0 / np.sqrt(8192.0))
LN_EPS = 1e-5


class Sem:
    __slots__ = ("h", "n")

    def __init__(self, h):
        self.h = h
        self.n = 0

    def mark(self):
        return (self, self.n)


class Tape:

    def __init__(self, nc, ctx):
        self.nc = nc
        self.tape = {e: [] for e in ENGS}
        self._ctx = ctx
        self._sems = {}
        self.wr = {}
        self.rd = {}

    def sem(self, name):
        if name not in self._sems:
            self._sems[name] = Sem(self._ctx.enter_context(self.nc.semaphore(name)))
        return self._sems[name]

    def op(self, eng, fn, wait=(), inc=(), dma=False, reads=(), writes=()):
        waits = []
        for w in wait:
            if isinstance(w, Sem):
                waits.append((w, w.n))
            else:
                waits.append((w[0], w[1]))
        for r in reads:
            m = self.wr.get(r)
            if m is not None:
                waits.append(m)
        for r in writes:
            m = self.wr.get(r)
            if m is not None:
                waits.append(m)
            for mm in self.rd.get(r, ()):
                waits.append(mm)
        byname = {}
        for s, v in waits:
            if v > 0 and (s not in byname or v > byname[s]):
                byname[s] = v
        waits = [(s, v) for s, v in byname.items()]
        incs = []
        for i in inc:
            if isinstance(i, Sem):
                incs.append((i, 16 if dma else 1))
            else:
                incs.append((i[0], i[1]))
        for s, k in incs:
            s.n += k
        self.tape[eng].append((waits, fn, incs))
        mark = (incs[-1][0], incs[-1][0].n) if incs else None
        if mark is not None:
            for r in writes:
                self.wr[r] = mark
                self.rd[r] = []
            for r in reads:
                self.rd.setdefault(r, []).append(mark)

    def note_read(self, r, mark):
        self.rd.setdefault(r, []).append(mark)

    def note_write(self, r, mark):
        self.wr[r] = mark
        self.rd[r] = []

    def raw_wait(self, eng, sem, val):
        self.tape[eng].append(([(sem, val)], None, []))

    def barrier(self, name):
        b = self.sem("bar_" + name)
        target = b.n + 5
        for e in ENGS:
            self.tape[e].append(([], lambda h: h.nop(), [(b, 1)]))
        b.n = target
        for e in ENGS:
            self.tape[e].append(([(b, target)], None, []))
        self.wr = {k: v for k, v in self.wr.items() if k.startswith("dma:")}
        self.rd = {}

    def replay(self, block):
        def mk(eng_name):
            def body(h):
                for waits, fn, incs in self.tape[eng_name]:
                    for s, v in waits:
                        h.wait_ge(s.h, v)
                    if fn is None:
                        continue
                    instr = fn(h)
                    for s, k in incs:
                        instr.then_inc(s.h, k)
            return body

        block.tensor(mk("tensor"))
        block.vector(mk("vector"))
        block.scalar(mk("scalar"))
        block.gpsimd(mk("gpsimd"))
        block.sync(mk("sync"))


def build_program():
    nc = bass.Bass("TRN2", target_bir_lowering=False)

    xa = nc.dram_tensor("xa", [NT, D], F32, kind="ExternalInput").ap()
    xb = nc.dram_tensor("xb", [NT, D], F32, kind="ExternalInput").ap()
    xbt = nc.dram_tensor("xbt", [D, NT], F32, kind="ExternalInput").ap()
    wqkvB = nc.dram_tensor("wqkvB", [24 * 128, 4096], F32, kind="ExternalInput").ap()
    wprojB = nc.dram_tensor("wprojB", [16 * 128, 2048], F32, kind="ExternalInput").ap()
    w1B = nc.dram_tensor("w1B", [64 * 128, 2048], F32, kind="ExternalInput").ap()
    w3B = nc.dram_tensor("w3B", [64 * 128, 2048], F32, kind="ExternalInput").ap()
    w2B = nc.dram_tensor("w2B", [16 * 128, 8192], F32, kind="ExternalInput").ap()
    c_id = nc.dram_tensor("c_id", [128, 128], BF16, kind="ExternalInput").ap()
    c_h = nc.dram_tensor("c_h", [128, 128], BF16, kind="ExternalInput").ap()
    c_m = nc.dram_tensor("c_m", [128, 128], BF16, kind="ExternalInput").ap()
    yT = nc.dram_tensor("yT", [D, NT], F32, kind="ExternalOutput").ap()
    s_v = nc.dram_tensor("s_v", [NT, D], BF16, kind="Internal").ap()
    s_hid = nc.dram_tensor("s_hid", [HF, NT], BF16, kind="Internal").ap()
    s_w2h = nc.dram_tensor("s_w2h", [D, HF], BF16, kind="Internal").ap()

    ctx = contextlib.ExitStack()
    sb = lambda name, shape, dt: ctx.enter_context(nc.sbuf_tensor(name, shape, dt))

    A1 = sb("A1", [128, 32768], BF16)
    A2 = sb("A2", [128, 32768], BF16)
    WB = sb("WB", [128, 8192], BF16)
    XF = sb("XF", [128, 8192], BF16)
    XC = sb("XC", [128, 4096], BF16)
    VINT = sb("VINT", [128, 4128], BF16)
    VST = sb("VST", [128, 512], BF16)
    SAB = sb("SAB", [128, 1024], BF16)
    SBB = sb("SBB", [128, 1024], BF16)
    GBUF = sb("GBUF", [128, 1024], BF16)
    HST = sb("HST", [128, 1024], BF16)
    YST = sb("YST", [128, 1024], F32)
    XBT = sb("XBT", [128, 1024], F32)
    EBUF = sb("EBUF", [128, 256], BF16)
    OBUF = sb("OBUF", [128, 256], BF16)
    RC = sb("RC", [128, 2], F32)
    DUMP = sb("DUMP", [128, 2048], BF16)
    SUMS = sb("SUMS", [128, 16], F32)
    SQS = sb("SQS", [128, 16], F32)
    MNEG = sb("MNEG", [128, 16], F32)
    RSTD = sb("RSTD", [128, 16], F32)
    TMPA = sb("TMPA", [128, 16], F32)
    TMPB = sb("TMPB", [128, 16], F32)
    CID = sb("CID", [128, 128], BF16)
    CH = sb("CH", [128, 128], BF16)
    CM = sb("CM", [128, 128], BF16)

    pb = [ctx.enter_context(nc.psum_tensor(f"pb{i}", [128, 512], F32)) for i in range(8)]
    pbf = [p[:, :].bitcast(BF16) for p in pb]
    XFf32 = XF[:, :].bitcast(F32)

    tp = Tape(nc, ctx)
    S = tp.sem
    CLD = S("CLD")

    for dst, src in ((CID, c_id), (CH, c_h), (CM, c_m)):
        tp.op("sync", lambda h, d=dst, s=src: h.dma_start(d[:, :], s), inc=[CLD], dma=True)
    cld = CLD.mark()
    ones_ap = bass.AP(VINT, 128, [[4128, 128], [129, 32], [1, 1]])
    tp.op("vector", lambda h: h.memset(ones_ap, 1.0), inc=[S("MISC")], writes=["vint_ones"])

    def emit_ln_tile(xsrc, tile, xn_col_base, P):
        b = tile % 2
        xld = XFf32[:, b * 2048:(b + 1) * 2048]
        xcb = XC[:, b * 2048:b * 2048 + 2048]
        col = tile % 16
        XL, XR, ST, VS, TT, TE = (S(P + "XL"), S(P + "XR"), S(P + "ST"),
                                  S(P + "VS"), S(P + "TT"), S(P + "TE"))
        tp.op("scalar", lambda h, x=xld, t=tile: h.dma_start(
            x, xsrc[128 * t:128 * (t + 1), :]), inc=[XL], dma=True,
            writes=[f"xf{b}"])
        tp.op("scalar", lambda h, x=xld, c=col: h.activation(
            DUMP[:, :], x, AF.Copy, bias=0.0, scale=np.float32(1.0),
            accum_out=SUMS[:, c:c + 1]), inc=[ST],
            reads=[f"xf{b}"], writes=[f"sum{col}"])
        tp.op("scalar", lambda h, x=xld, c=col: h.activation(
            DUMP[:, :], x, AF.Square, accum_out=SQS[:, c:c + 1]), inc=[ST],
            reads=[f"xf{b}"], writes=[f"sq{col}"])
        tp.op("vector", lambda h, c=col: h.tensor_scalar(
            MNEG[:, c:c + 1], SUMS[:, c:c + 1], float(-1.0 / 2048.0), None,
            ALU.mult), inc=[VS], reads=[f"sum{col}"], writes=[f"mneg{col}"])
        tp.op("vector", lambda h, c=col: h.tensor_scalar(
            TMPA[:, c:c + 1], SQS[:, c:c + 1], float(1.0 / 2048.0), None,
            ALU.mult), inc=[VS], reads=[f"sq{col}"], writes=[f"ta{col}"])
        tp.op("vector", lambda h, c=col: h.tensor_mul(
            TMPB[:, c:c + 1], MNEG[:, c:c + 1], MNEG[:, c:c + 1]), inc=[VS],
            reads=[f"mneg{col}"], writes=[f"tb{col}"])
        tp.op("vector", lambda h, c=col: h.tensor_sub(
            TMPA[:, c:c + 1], TMPA[:, c:c + 1], TMPB[:, c:c + 1]), inc=[VS],
            reads=[f"tb{col}"], writes=[f"ta{col}"])
        tp.op("scalar", lambda h, c=col: h.activation(
            TMPB[:, c:c + 1], TMPA[:, c:c + 1], AF.Ln, bias=np.float32(LN_EPS)),
            inc=[ST], reads=[f"ta{col}"], writes=[f"tb{col}"])
        tp.op("scalar", lambda h, c=col: h.activation(
            RSTD[:, c:c + 1], TMPB[:, c:c + 1], AF.Exp, scale=np.float32(-0.5)),
            inc=[ST], reads=[f"tb{col}"], writes=[f"rstd{col}"])
        tp.op("vector", lambda h, x=xld, xc=xcb, c=col: h.tensor_scalar(
            xc, x, MNEG[:, c:c + 1], RSTD[:, c:c + 1], ALU.add, ALU.mult),
            inc=[XR], reads=[f"xf{b}", f"mneg{col}", f"rstd{col}"],
            writes=[f"xc{b}"])
        for dt in range(16):
            bank = 6 + dt % 2
            tp.op("tensor", lambda h, xc=xcb, d=dt, bk=bank: h.transpose(
                pbf[bk][:, 0:128], xc[:, d * 128:(d + 1) * 128], CID[:, :]),
                wait=[cld], inc=[TT], reads=[f"xc{b}"], writes=[f"pb{bank}"])
            dcol = xn_col_base(dt, tile)
            tp.op("vector", lambda h, bk=bank, c=dcol: h.tensor_copy(
                A2[:, c:c + 128], pbf[bk][:, 0:128]), inc=[TE],
                reads=[f"pb{bank}"], writes=["A2lo" if P == "A" else "A2"])
        return TE.mark()

    WLD, VMM, VEV, SVW = S("WLD"), S("VMM"), S("VEV"), S("SVW")
    QKM, QEV, VGL = S("QKM"), S("QEV"), S("VGL")
    GQK, GEX, GMS, GAV, GRC, GTS, GTP, GPT = (
        S("GQK"), S("GEX"), S("GMS"), S("GAV"), S("GRC"), S("GTS"), S("GTP"), S("GPT"))

    for half in range(2):
        for t8 in range(8):
            emit_ln_tile(xa, half * 8 + t8,
                         lambda dt, t: dt * 1024 + (t % 8) * 128, "A")
        for mc in range(8):
            wbb = mc % 2
            src = wqkvB[(16 + mc) * 128:(17 + mc) * 128, :]
            dst = WB[:, wbb * 4096:(wbb + 1) * 4096]
            tp.op("gpsimd", lambda h, d=dst, s=src: h.dma_start(d, s),
                  inc=[WLD], dma=True, writes=[f"wb{wbb}"])
            for tt in range(8):
                bank = (mc * 8 + tt) % 2
                for dt in range(16):
                    tp.op("tensor", lambda h, b=bank, d=dt, t=tt, w=wbb: h.matmul(
                        pb[b][:, 0:256],
                        A2[:, d * 1024 + t * 128: d * 1024 + (t + 1) * 128],
                        WB[:, w * 4096 + d * 256: w * 4096 + d * 256 + 256],
                        start=(d == 0), stop=(d == 15)),
                        inc=([VMM] if dt == 15 else []),
                        reads=(["A2lo", f"wb{wbb}"] if dt == 0 else []),
                        writes=([f"pb{bank}"] if dt == 0 else []))
                vm = VMM.mark()
                tp.note_write(f"pb{bank}", vm)
                tp.note_read("A2lo", vm)
                tp.note_read(f"wb{wbb}", vm)
                tp.op("vector", lambda h, b=bank: h.tensor_copy(
                    VST[:, b * 256:(b + 1) * 256], pb[b][:, 0:256]),
                    inc=[VEV], reads=[f"pb{bank}"], writes=[f"vst{bank}"])
                tp.op("sync", lambda h, b=bank, t=tt, m=mc, hf=half: h.dma_start(
                    s_v[128 * (hf * 8 + t):128 * (hf * 8 + t + 1),
                        m * 256:(m + 1) * 256],
                    VST[:, b * 256:(b + 1) * 256]),
                    inc=[SVW], dma=True, reads=[f"vst{bank}"])
        v_done = SVW.mark()
        for qr in range(2):
            qq = half * 2 + qr
            for mc in range(16):
                wbb = mc % 2
                src = wqkvB[mc * 128:(mc + 1) * 128, :]
                dst = WB[:, wbb * 4096:(wbb + 1) * 4096]
                tp.op("gpsimd", lambda h, d=dst, s=src: h.dma_start(d, s),
                      inc=[WLD], dma=True, writes=[f"wb{wbb}"])
                for mt2 in range(2):
                    mt = 2 * mc + mt2
                    bank = mt % 4
                    for dt in range(16):
                        tp.op("tensor", lambda h, b=bank, d=dt, q=qr, w=wbb, m2=mt2: h.matmul(
                            pb[b][:, 0:512],
                            WB[:, w * 4096 + d * 256 + m2 * 128:
                               w * 4096 + d * 256 + m2 * 128 + 128],
                            A2[:, d * 1024 + q * 512: d * 1024 + (q + 1) * 512],
                            start=(d == 0), stop=(d == 15)),
                            inc=([QKM] if dt == 15 else []),
                            reads=(["A2lo", f"wb{wbb}"] if dt == 0 else []),
                            writes=([f"pb{bank}"] if dt == 0 else []))
                    qm = QKM.mark()
                    tp.note_write(f"pb{bank}", qm)
                    tp.note_read("A2lo", qm)
                    tp.note_read(f"wb{wbb}", qm)
                    i = mt % 16
                    base = 16384 + (8192 if mt >= 16 else 0) + i
                    dstq = bass.AP(A2, base, [[32768, 128], [128, 64], [16, 8]])
                    srcq = bass.AP(pb[bank], 0, [[512, 128], [8, 64], [1, 8]])
                    tp.op("vector", lambda h, d=dstq, s=srcq: h.tensor_copy(d, s),
                          inc=[QEV], reads=[f"pb{bank}"],
                          writes=["rQI" if mt < 16 else "rKI"])
            qk_done = QEV.mark()
            for tt4 in range(4):
                TTI = 4 * qq + tt4
                vb = tt4 % 2
                for tau in range(8):
                    dstv = bass.AP(VINT, tau * 16 * 4128 + vb * 2064,
                                   [[4128, 16], [129, 16], [1, 128]])
                    srcv = bass.AP(s_v.tensor, (128 * TTI + tau) * 2048,
                                   [[128, 16], [8 * 2048, 16], [1, 128]])
                    tp.op("sync", lambda h, d=dstv, s=srcv: h.dma_start(d, s),
                          wait=([v_done] if tau == 0 else []), inc=[VGL], dma=True,
                          reads=(["vint_ones"] if tau == 0 else []),
                          writes=([f"vint{vb}"] if tau == 0 else []),
                          )
                vload = VGL.mark()
                tp.note_write(f"vint{vb}", vload)
                for g16 in range(16):
                    gq = tt4 * 16 + g16
                    Gg = 64 * qq + gq
                    p2 = g16 % 2
                    tp.op("tensor", lambda h, g=gq, b=p2: h.matmul(
                        pb[b][:, 0:128],
                        A2[:, 24576 + g * 128: 24576 + (g + 1) * 128],
                        A2[:, 16384 + g * 128: 16384 + (g + 1) * 128],
                        start=True, stop=True),
                        wait=[qk_done], inc=[GQK],
                        reads=["rQI", "rKI"], writes=[f"pb{p2}"])
                    tp.op("scalar", lambda h, b=p2: h.activation(
                        EBUF[:, b * 128:(b + 1) * 128], pb[b][:, 0:128], AF.Exp,
                        bias=np.float32(-8.0), scale=np.float32(INV_SQRT_HD)),
                        inc=[GEX], reads=[f"pb{p2}"], writes=[f"eb{p2}"])
                    tp.op("vector", lambda h, b=p2: h.tensor_mul(
                        EBUF[:, b * 128:(b + 1) * 128],
                        EBUF[:, b * 128:(b + 1) * 128], CM[:, :]),
                        wait=[cld], inc=[GMS], reads=[f"eb{p2}"], writes=[f"eb{p2}"])
                    tp.op("tensor", lambda h, b=p2, g=g16, v=vb: h.matmul(
                        pb[2 + b][:, 0:129],
                        EBUF[:, b * 128:(b + 1) * 128],
                        VINT[:, v * 2064 + g * 129: v * 2064 + (g + 1) * 129],
                        start=True, stop=True),
                        wait=[vload], inc=[GAV],
                        reads=[f"eb{p2}", f"vint{vb}"], writes=[f"pb{2 + p2}"])
                    tp.op("vector", lambda h, b=p2: h.reciprocal(
                        RC[:, b:b + 1], pb[2 + b][:, 128:129]),
                        inc=[GRC], reads=[f"pb{2 + p2}"], writes=[f"rc{p2}"])
                    tp.op("vector", lambda h, b=p2: h.tensor_scalar(
                        OBUF[:, b * 128:(b + 1) * 128], pb[2 + b][:, 0:128],
                        RC[:, b:b + 1], None, ALU.mult),
                        inc=[GTS], reads=[f"pb{2 + p2}", f"rc{p2}"],
                        writes=[f"ob{p2}"])
                    tp.op("tensor", lambda h, b=p2: h.transpose(
                        pbf[4 + b][:, 0:128], OBUF[:, b * 128:(b + 1) * 128],
                        CID[:, :]),
                        inc=[GTP], reads=[f"ob{p2}"], writes=[f"pb{4 + p2}"])
                    gg = Gg // 2
                    ctb = 8 * (Gg % 2)
                    dstp = bass.AP(A1, ctb * 2048 + gg,
                                   [[32768, 128], [2048, 8], [128, 16]])
                    srcp = pbf[4 + p2][:, 0:128].rearrange("p (t i) -> p t i", t=8)
                    tp.op("vector", lambda h, d=dstp, s=srcp: h.tensor_copy(d, s),
                          inc=[GPT], reads=[f"pb{4 + p2}"])

    tp.barrier("AB")

    for tile in range(16):
        emit_ln_tile(xb, tile, lambda dt, t: dt * 2048 + t * 128, "B")
    tp.barrier("B0")

    BMA, BMB, SIL, ULC, GTM, ZMM, ZEV, HSW = (
        S("BMA"), S("BMB"), S("SIL"), S("ULC"), S("GTM"), S("ZMM"), S("ZEV"),
        S("HSW"))
    BLD, BFY, BSW = S("BLD"), S("BFY"), S("BSW")

    bfly_q = []

    def queue_butterfly_chunk(bc):
        srcl = w2B[bc * 128:(bc + 1) * 128, :]
        bfly_q.append(lambda s=srcl: tp.op(
            "gpsimd", lambda h, d=XF[:, :], ss=s: h.dma_start(d, ss),
            inc=[BLD], dma=True, writes=["bfA"], reads=["dma:bsw"]))
        for st in range(6):
            s2 = 1 << st
            na = 32 // s2
            inner = s2 * 128
            a_lo = bass.AP(XF, 0, [[8192, 128], [2 * inner, na], [1, inner]])
            a_hi = bass.AP(XF, inner, [[8192, 128], [2 * inner, na], [1, inner]])
            xct = bass.AP(XC, 0, [[4096, 128], [inner, na], [1, inner]])
            bfly_q.append(lambda o=xct, a=a_lo: tp.op(
                "vector", lambda h: h.tensor_copy(o, a),
                inc=[BFY], reads=["bfA"], writes=["xctmp"]))
            bfly_q.append(lambda o=a_lo, a=a_lo, b=a_hi: tp.op(
                "vector", lambda h: h.tensor_add(o, a, b),
                inc=[BFY], reads=["bfA"], writes=["bfA"]))
            bfly_q.append(lambda o=a_hi, a=xct, b=a_hi: tp.op(
                "vector", lambda h: h.tensor_sub(o, a, b),
                inc=[BFY], reads=["xctmp", "bfA"], writes=["bfA"]))
        dsts = s_w2h[bc * 128:(bc + 1) * 128, :]
        bfly_q.append(lambda d=dsts: tp.op(
            "sync", lambda h: h.dma_start(d, XF[:, :]),
            inc=[BSW], dma=True, reads=["bfA"], writes=["dma:bsw"]))

    for _bc in range(16):
        queue_butterfly_chunk(_bc)

    pending_z = []

    def flush_z():
        while pending_z:
            pending_z.pop(0)()

    for ht in range(64):
        hb = ht % 2
        for widx, wt in enumerate((w1B, w3B)):
            src = wt[ht * 128:(ht + 1) * 128, :]
            qoff = hb * 4096 + widx * 2048
            dst = WB[:, qoff:qoff + 2048]
            tp.op("gpsimd", lambda h, d=dst, s=src: h.dma_start(d, s),
                  inc=[WLD], dma=True, writes=[f"wq{hb}{widx}"])
        for rb in range(4):
            flush_z()
            for _ in range(2):
                if bfly_q:
                    bfly_q.pop(0)()
            bA = rb % 2
            bB = 2 + rb % 2
            for widx, (bk, msem) in enumerate(((bA, BMA), (bB, BMB))):
                qoff = hb * 4096 + widx * 2048
                for dt in range(16):
                    tp.op("tensor", lambda h, b=bk, d=dt, qo=qoff, r=rb: h.matmul(
                        pb[b][:, 0:512],
                        WB[:, qo + d * 128: qo + (d + 1) * 128],
                        A2[:, d * 2048 + r * 512: d * 2048 + (r + 1) * 512],
                        start=(d == 0), stop=(d == 15)),
                        inc=([msem] if dt == 15 else []),
                        reads=(["A2", f"wq{hb}{widx}"] if dt == 0 else []),
                        writes=([f"pb{bk}"] if dt == 0 else []))
                bm = msem.mark()
                tp.note_write(f"pb{bk}", bm)
                tp.note_read("A2", bm)
                tp.note_read(f"wq{hb}{widx}", bm)
            tp.op("scalar", lambda h, b=bA, r=rb: h.activation(
                SAB[:, (r % 2) * 512:(r % 2) * 512 + 512], pb[b][:, 0:512],
                AF.Silu), inc=[SIL], reads=[f"pb{bA}"], writes=[f"sab{rb % 2}"])
            tp.op("scalar", lambda h, b=bB, r=rb: h.activation(
                SBB[:, (r % 2) * 512:(r % 2) * 512 + 512], pb[b][:, 0:512],
                AF.Copy, bias=0.0, scale=np.float32(FWHT_SCALE)),
                inc=[ULC], reads=[f"pb{bB}"], writes=[f"sbb{rb % 2}"])
            tp.op("vector", lambda h, r=rb: h.tensor_mul(
                GBUF[:, (r % 2) * 512:(r % 2) * 512 + 512],
                SAB[:, (r % 2) * 512:(r % 2) * 512 + 512],
                SBB[:, (r % 2) * 512:(r % 2) * 512 + 512]),
                inc=[GTM], reads=[f"sab{rb % 2}", f"sbb{rb % 2}"],
                writes=[f"gb{rb % 2}"])
            zb = 4 + rb % 2

            def _zops(r=rb, z=zb, t=ht):
                tp.op("tensor", lambda h: h.matmul(
                    pb[z][:, 0:512], CH[:, :],
                    GBUF[:, (r % 2) * 512:(r % 2) * 512 + 512],
                    start=True, stop=True), wait=[cld],
                    inc=[ZMM], reads=[f"gb{r % 2}"], writes=[f"pb{z}"])
                tp.op("vector", lambda h: h.tensor_copy(
                    HST[:, (r % 2) * 512:(r % 2) * 512 + 512], pb[z][:, 0:512]),
                    inc=[ZEV], reads=[f"pb{z}"], writes=[f"hst{r % 2}"])
                tp.op("scalar", lambda h: h.dma_start(
                    s_hid[128 * t:128 * (t + 1), r * 512:(r + 1) * 512],
                    HST[:, (r % 2) * 512:(r % 2) * 512 + 512]),
                    inc=[HSW], dma=True, reads=[f"hst{r % 2}"], writes=["dma:hsw"])

            pending_z.append(_zops)
            if len(pending_z) > 1:
                pending_z.pop(0)()
    flush_z()
    while bfly_q:
        bfly_q.pop(0)()

    tp.barrier("BC")
    hid_done = HSW.mark()
    w2h_done = BSW.mark()

    HQL, WCL, PJL, XBL, CMM, YEV, YSW = (
        S("HQL"), S("WCL"), S("PJL"), S("XBL"), S("CMM"), S("YEV"), S("YSW"))
    for q in range(4):
        srch = s_hid.rearrange("(ht p) r -> p ht r", p=128)[
            :, :, q * 512:(q + 1) * 512]
        dsth = A2[:, :].rearrange("p (ht r) -> p ht r", ht=64)
        tp.op("sync", lambda h, d=dsth, s=srch: h.dma_start(d, s),
              wait=[hid_done], inc=[HQL], dma=True, writes=["A2"])
        for dct in range(16):
            wb2 = dct % 2
            wcol = WB[:, :] if wb2 == 0 else XF[:, :]
            wcol_res = "wb_c" if wb2 == 0 else "xf_c"
            srcw = s_w2h[dct * 128:(dct + 1) * 128, :]
            tp.op("sync", lambda h, d=wcol, s=srcw: h.dma_start(d, s),
                  wait=[w2h_done], inc=[WCL], dma=True, writes=[wcol_res])
            srcp = wprojB[dct * 128:(dct + 1) * 128, :]
            dstp = XC[:, wb2 * 2048:(wb2 + 1) * 2048]
            tp.op("gpsimd", lambda h, d=dstp, s=srcp: h.dma_start(d, s),
                  inc=[PJL], dma=True, writes=[f"xc{wb2}"])
            tp.op("scalar", lambda h, d=dct, qi=q, w=wb2: h.dma_start(
                XBT[:, w * 512:(w + 1) * 512],
                xbt[128 * d:128 * (d + 1), qi * 512:(qi + 1) * 512]),
                inc=[XBL], dma=True, writes=[f"xbt{wb2}"])
            bko = dct % 2
            for ct in range(16):
                tp.op("tensor", lambda h, c=ct, w=wb2, qi=q, b=bko: h.matmul(
                    pb[b][:, 0:512],
                    XC[:, w * 2048 + c * 128: w * 2048 + (c + 1) * 128],
                    A1[:, c * 2048 + qi * 512: c * 2048 + qi * 512 + 512],
                    start=(c == 0), stop=False),
                    reads=([f"xc{wb2}", wcol_res, "A2"] if ct == 0 else []),
                    writes=([f"pb{bko}"] if ct == 0 else []))
            for C in range(64):
                tp.op("tensor", lambda h, c=C, w=wcol, b=bko: h.matmul(
                    pb[b][:, 0:512],
                    w[:, c * 128:(c + 1) * 128],
                    A2[:, c * 512:(c + 1) * 512],
                    start=False, stop=(c == 63)),
                    inc=([CMM] if C == 63 else []))
            cm2 = CMM.mark()
            tp.note_write(f"pb{bko}", cm2)
            tp.note_read(f"xc{wb2}", cm2)
            tp.note_read(wcol_res, cm2)
            tp.note_read("A2", cm2)
            tp.op("vector", lambda h, b=bko, w=wb2: h.tensor_add(
                YST[:, w * 512:(w + 1) * 512], pb[b][:, 0:512],
                XBT[:, w * 512:(w + 1) * 512]),
                inc=[YEV], reads=[f"pb{bko}", f"xbt{wb2}"], writes=[f"yst{wb2}"])
            tp.op("scalar", lambda h, d=dct, qi=q, w=wb2: h.dma_start(
                yT[128 * d:128 * (d + 1), qi * 512:(qi + 1) * 512],
                YST[:, w * 512:(w + 1) * 512]),
                inc=[YSW], dma=True, reads=[f"yst{wb2}"])
    tp.raw_wait("sync", YSW, YSW.n)

    with nc.Block() as block:
        tp.replay(block)
    ctx.close()
    return nc


_NC_CACHE = {}


def _get_program():
    if "nc" not in _NC_CACHE:
        _NC_CACHE["nc"] = build_program()
    return _NC_CACHE["nc"]


def _consts():
    H = np.array([[1.0]], np.float32)
    while H.shape[0] < 128:
        H = np.block([[H, H], [H, -H]])
    cid = np.eye(128, dtype=BF)
    cm = np.zeros((128, 128), dtype=BF)
    for t in range(8):
        cm[16 * t:16 * t + 16, 16 * t:16 * t + 16] = 1.0
    return cid, H.astype(BF), cm


def kernel(L0_attn_x, L0_attn_w_qkv, L0_attn_w_proj, L0_ffn_W1, L0_ffn_W2,
           L0_ffn_W3):
    X = np.ascontiguousarray(np.asarray(L0_attn_x, dtype=np.float32)[0])
    wqkvT = np.asarray(L0_attn_w_qkv, np.float32).T
    wprojT = np.asarray(L0_attn_w_proj, np.float32).T
    w1T = np.asarray(L0_ffn_W1, np.float32).T
    w3T = np.asarray(L0_ffn_W3, np.float32).T
    w2T = np.asarray(L0_ffn_W2, np.float32).T
    wqkvB = np.ascontiguousarray(
        wqkvT.reshape(16, 128, 24, 256).transpose(2, 1, 0, 3)).reshape(24 * 128, 4096)
    wprojB = np.ascontiguousarray(
        wprojT.reshape(16, 128, 16, 128).transpose(2, 1, 0, 3)).reshape(16 * 128, 2048)
    w1B = np.ascontiguousarray(
        w1T.reshape(16, 128, 64, 128).transpose(2, 1, 0, 3)).reshape(64 * 128, 2048)
    w3B = np.ascontiguousarray(
        w3T.reshape(16, 128, 64, 128).transpose(2, 1, 0, 3)).reshape(64 * 128, 2048)
    w2B = np.ascontiguousarray(
        w2T.reshape(64, 128, 16, 128).transpose(2, 1, 0, 3)).reshape(16 * 128, 8192)
    cid, ch, cm = _consts()

    in_maps = []
    s_list = []
    for c in range(8):
        Sidx = np.array([h * 1024 + 128 * c + gg for h in range(16)
                         for gg in range(128)])
        s_list.append(Sidx)
        xbv = np.ascontiguousarray(X[Sidx])
        in_maps.append(dict(
            xa=np.ascontiguousarray(X[2048 * c:2048 * (c + 1)]),
            xb=xbv,
            xbt=np.ascontiguousarray(xbv.T),
            wqkvB=wqkvB, wprojB=wprojB, w1B=w1B, w3B=w3B, w2B=w2B,
            c_id=cid, c_h=ch, c_m=cm,
        ))

    nc = _get_program()
    res = run_bass_kernel_spmd(nc, in_maps, core_ids=list(range(8)))
    out = np.empty((16384, 2048), np.float32)
    for c in range(8):
        out[s_list[c]] = res.results[c]["yT"].T
    return out.reshape(1, 16384, 2048)


# version 05
# speedup vs baseline: 1.3725x; 1.0824x over previous
import contextlib
import numpy as np
import ml_dtypes
import concourse.bass as bass
import concourse.mybir as mybir
from concourse.bass_utils import run_bass_kernel_spmd

F32 = mybir.dt.float32
BF16 = mybir.dt.bfloat16
AF = mybir.ActivationFunctionType
ALU = mybir.AluOpType
BF = ml_dtypes.bfloat16

ENGS = ("tensor", "vector", "scalar", "gpsimd", "sync")
D = 2048
NT = 2048
MQKV = 6144
HF = 8192
INV_SQRT_HD = float(1.0 / np.sqrt(128.0))
FWHT_SCALE = float(1.0 / np.sqrt(8192.0))
LN_EPS = 1e-5


class Sem:
    __slots__ = ("h", "n")

    def __init__(self, h):
        self.h = h
        self.n = 0

    def mark(self):
        return (self, self.n)


class Tape:

    def __init__(self, nc, ctx):
        self.nc = nc
        self.tape = {e: [] for e in ENGS}
        self._ctx = ctx
        self._sems = {}
        self.wr = {}
        self.rd = {}

    def sem(self, name):
        if name not in self._sems:
            self._sems[name] = Sem(self._ctx.enter_context(self.nc.semaphore(name)))
        return self._sems[name]

    def op(self, eng, fn, wait=(), inc=(), dma=False, reads=(), writes=()):
        waits = []
        for w in wait:
            if isinstance(w, Sem):
                waits.append((w, w.n))
            else:
                waits.append((w[0], w[1]))
        for r in reads:
            m = self.wr.get(r)
            if m is not None:
                waits.append(m)
        for r in writes:
            m = self.wr.get(r)
            if m is not None:
                waits.append(m)
            for mm in self.rd.get(r, ()):
                waits.append(mm)
        byname = {}
        for s, v in waits:
            if v > 0 and (s not in byname or v > byname[s]):
                byname[s] = v
        waits = [(s, v) for s, v in byname.items()]
        incs = []
        for i in inc:
            if isinstance(i, Sem):
                incs.append((i, 16 if dma else 1))
            else:
                incs.append((i[0], i[1]))
        for s, k in incs:
            s.n += k
        self.tape[eng].append((waits, fn, incs))
        mark = (incs[-1][0], incs[-1][0].n) if incs else None
        if mark is not None:
            for r in writes:
                self.wr[r] = mark
                self.rd[r] = []
            for r in reads:
                self.rd.setdefault(r, []).append(mark)

    def note_read(self, r, mark):
        self.rd.setdefault(r, []).append(mark)

    def note_write(self, r, mark):
        self.wr[r] = mark
        self.rd[r] = []

    def raw_wait(self, eng, sem, val):
        self.tape[eng].append(([(sem, val)], None, []))

    def barrier(self, name):
        b = self.sem("bar_" + name)
        target = b.n + 5
        for e in ENGS:
            self.tape[e].append(([], lambda h: h.nop(), [(b, 1)]))
        b.n = target
        for e in ENGS:
            self.tape[e].append(([(b, target)], None, []))
        self.wr = {k: v for k, v in self.wr.items() if k.startswith("dma:")}
        self.rd = {}

    def replay(self, block):
        def mk(eng_name):
            def body(h):
                for waits, fn, incs in self.tape[eng_name]:
                    for s, v in waits:
                        h.wait_ge(s.h, v)
                    if fn is None:
                        continue
                    instr = fn(h)
                    for s, k in incs:
                        instr.then_inc(s.h, k)
            return body

        block.tensor(mk("tensor"))
        block.vector(mk("vector"))
        block.scalar(mk("scalar"))
        block.gpsimd(mk("gpsimd"))
        block.sync(mk("sync"))


def build_program():
    nc = bass.Bass("TRN2", target_bir_lowering=False)

    xa = nc.dram_tensor("xa", [NT, D], F32, kind="ExternalInput").ap()
    xb = nc.dram_tensor("xb", [NT, D], F32, kind="ExternalInput").ap()
    xbt = nc.dram_tensor("xbt", [D, NT], F32, kind="ExternalInput").ap()
    wqkvB = nc.dram_tensor("wqkvB", [24 * 128, 4096], F32, kind="ExternalInput").ap()
    wprojB = nc.dram_tensor("wprojB", [16 * 128, 2048], F32, kind="ExternalInput").ap()
    w1B = nc.dram_tensor("w1B", [64 * 128, 2048], F32, kind="ExternalInput").ap()
    w3B = nc.dram_tensor("w3B", [64 * 128, 2048], F32, kind="ExternalInput").ap()
    w2B = nc.dram_tensor("w2B", [16 * 128, 8192], F32, kind="ExternalInput").ap()
    c_id = nc.dram_tensor("c_id", [128, 128], BF16, kind="ExternalInput").ap()
    c_h = nc.dram_tensor("c_h", [128, 128], BF16, kind="ExternalInput").ap()
    c_m = nc.dram_tensor("c_m", [128, 128], BF16, kind="ExternalInput").ap()
    c_m4 = nc.dram_tensor("c_m4", [128, 512], BF16, kind="ExternalInput").ap()
    yT = nc.dram_tensor("yT", [D, NT], F32, kind="ExternalOutput").ap()
    s_v = nc.dram_tensor("s_v", [NT, D], BF16, kind="Internal").ap()
    s_wq = nc.dram_tensor("s_wq", [24 * 128, 4096], BF16, kind="Internal").ap()
    s_hid = nc.dram_tensor("s_hid", [HF, NT], BF16, kind="Internal").ap()
    s_w2h = nc.dram_tensor("s_w2h", [D, HF], BF16, kind="Internal").ap()

    ctx = contextlib.ExitStack()
    sb = lambda name, shape, dt: ctx.enter_context(nc.sbuf_tensor(name, shape, dt))

    A1 = sb("A1", [128, 32768], BF16)
    A2 = sb("A2", [128, 32768], BF16)
    WB = sb("WB", [128, 8192], BF16)
    XF = sb("XF", [128, 8192], BF16)
    XC = sb("XC", [128, 4096], BF16)
    VINT = sb("VINT", [128, 4128], BF16)
    VST = sb("VST", [128, 512], BF16)
    SAB = sb("SAB", [128, 1024], BF16)
    SBB = sb("SBB", [128, 1024], BF16)
    GBUF = sb("GBUF", [128, 1024], BF16)
    HST = sb("HST", [128, 1024], BF16)
    YST = sb("YST", [128, 1024], F32)
    XBT = sb("XBT", [128, 1024], F32)
    EBUF = sb("EBUF", [128, 1024], BF16)
    OBUF = sb("OBUF", [128, 1024], BF16)
    RC = sb("RC", [128, 4], F32)
    DUMP = sb("DUMP", [128, 2048], BF16)
    SUMS = sb("SUMS", [128, 16], F32)
    SQS = sb("SQS", [128, 16], F32)
    MNEG = sb("MNEG", [128, 16], F32)
    RSTD = sb("RSTD", [128, 16], F32)
    TMPA = sb("TMPA", [128, 16], F32)
    TMPB = sb("TMPB", [128, 16], F32)
    CID = sb("CID", [128, 128], BF16)
    CH = sb("CH", [128, 128], BF16)
    CM = sb("CM", [128, 128], BF16)
    CM4 = sb("CM4", [128, 512], BF16)

    pb = [ctx.enter_context(nc.psum_tensor(f"pb{i}", [128, 512], F32)) for i in range(8)]
    pbf = [p[:, :].bitcast(BF16) for p in pb]
    XFf32 = XF[:, :].bitcast(F32)

    tp = Tape(nc, ctx)
    S = tp.sem
    CLD = S("CLD")

    for dst, src in ((CID, c_id), (CH, c_h), (CM, c_m), (CM4, c_m4)):
        tp.op("sync", lambda h, d=dst, s=src: h.dma_start(d[:, :], s), inc=[CLD], dma=True)
    cld = CLD.mark()
    ones_ap = bass.AP(VINT, 128, [[4128, 128], [129, 32], [1, 1]])
    tp.op("vector", lambda h: h.memset(ones_ap, 1.0), inc=[S("MISC")], writes=["vint_ones"])

    def emit_ln_tile(xsrc, tile, xn_col_base, P):
        b = tile % 2
        xld = XFf32[:, b * 2048:(b + 1) * 2048]
        xcb = XC[:, b * 2048:b * 2048 + 2048]
        col = tile % 16
        XL, XR, ST, VS, TT, TE = (S(P + "XL"), S(P + "XR"), S(P + "ST"),
                                  S(P + "VS"), S(P + "TT"), S(P + "TE"))
        tp.op("scalar", lambda h, x=xld, t=tile: h.dma_start(
            x, xsrc[128 * t:128 * (t + 1), :]), inc=[XL], dma=True,
            writes=[f"xf{b}"])
        tp.op("scalar", lambda h, x=xld, c=col: h.activation(
            DUMP[:, :], x, AF.Copy, bias=0.0, scale=np.float32(1.0),
            accum_out=SUMS[:, c:c + 1]), inc=[ST],
            reads=[f"xf{b}"], writes=[f"sum{col}"])
        tp.op("scalar", lambda h, x=xld, c=col: h.activation(
            DUMP[:, :], x, AF.Square, accum_out=SQS[:, c:c + 1]), inc=[ST],
            reads=[f"xf{b}"], writes=[f"sq{col}"])
        tp.op("vector", lambda h, c=col: h.tensor_scalar(
            MNEG[:, c:c + 1], SUMS[:, c:c + 1], float(-1.0 / 2048.0), None,
            ALU.mult), inc=[VS], reads=[f"sum{col}"], writes=[f"mneg{col}"])
        tp.op("vector", lambda h, c=col: h.tensor_scalar(
            TMPA[:, c:c + 1], SQS[:, c:c + 1], float(1.0 / 2048.0), None,
            ALU.mult), inc=[VS], reads=[f"sq{col}"], writes=[f"ta{col}"])
        tp.op("vector", lambda h, c=col: h.tensor_mul(
            TMPB[:, c:c + 1], MNEG[:, c:c + 1], MNEG[:, c:c + 1]), inc=[VS],
            reads=[f"mneg{col}"], writes=[f"tb{col}"])
        tp.op("vector", lambda h, c=col: h.tensor_sub(
            TMPA[:, c:c + 1], TMPA[:, c:c + 1], TMPB[:, c:c + 1]), inc=[VS],
            reads=[f"tb{col}"], writes=[f"ta{col}"])
        tp.op("scalar", lambda h, c=col: h.activation(
            TMPB[:, c:c + 1], TMPA[:, c:c + 1], AF.Ln, bias=np.float32(LN_EPS)),
            inc=[ST], reads=[f"ta{col}"], writes=[f"tb{col}"])
        tp.op("scalar", lambda h, c=col: h.activation(
            RSTD[:, c:c + 1], TMPB[:, c:c + 1], AF.Exp, scale=np.float32(-0.5)),
            inc=[ST], reads=[f"tb{col}"], writes=[f"rstd{col}"])
        tp.op("vector", lambda h, x=xld, xc=xcb, c=col: h.tensor_scalar(
            xc, x, MNEG[:, c:c + 1], RSTD[:, c:c + 1], ALU.add, ALU.mult),
            inc=[XR], reads=[f"xf{b}", f"mneg{col}", f"rstd{col}"],
            writes=[f"xc{b}"])
        for dt in range(16):
            bank = 6 + dt % 2
            tp.op("tensor", lambda h, xc=xcb, d=dt, bk=bank: h.transpose(
                pbf[bk][:, 0:128], xc[:, d * 128:(d + 1) * 128], CID[:, :]),
                wait=[cld], inc=[TT], reads=[f"xc{b}"], writes=[f"pb{bank}"])
            dcol = xn_col_base(dt, tile)
            tp.op("vector", lambda h, bk=bank, c=dcol: h.tensor_copy(
                A2[:, c:c + 128], pbf[bk][:, 0:128]), inc=[TE],
                reads=[f"pb{bank}"], writes=["A2lo" if P == "A" else "A2"])
        return TE.mark()

    WLD, VMM, VEV, SVW = S("WLD"), S("VMM"), S("VEV"), S("SVW")
    WQW = S("WQW")
    wq_cached = {}
    QKM, QEV, VGL = S("QKM"), S("QEV"), S("VGL")
    GQK, GEX, GMS, GAV, GRC, GTS, GTP, GPT = (
        S("GQK"), S("GEX"), S("GMS"), S("GAV"), S("GRC"), S("GTS"), S("GTP"), S("GPT"))

    ln_marks = []
    for half in range(2):
        for t8 in range(8):
            ln_marks.append(emit_ln_tile(xa, half * 8 + t8,
                            lambda dt, t: dt * 1024 + (t % 8) * 128, "A"))
        for mc in range(8):
            wbb = mc % 2
            blk = 16 + mc
            dst = WB[:, wbb * 4096:(wbb + 1) * 4096]
            if blk not in wq_cached:
                src = wqkvB[blk * 128:(blk + 1) * 128, :]
                tp.op("gpsimd", lambda h, d=dst, s=src: h.dma_start(d, s),
                      inc=[WLD], dma=True, writes=[f"wb{wbb}"])
                tp.op("sync", lambda h, d=s_wq[blk * 128:(blk + 1) * 128, :],
                      s=dst: h.dma_start(d, s),
                      inc=[WQW], dma=True, reads=[f"wb{wbb}"])
                wq_cached[blk] = WQW.mark()
            else:
                tp.op("sync", lambda h, d=dst,
                      s=s_wq[blk * 128:(blk + 1) * 128, :]: h.dma_start(d, s),
                      wait=[wq_cached[blk]],
                      inc=[WLD], dma=True, writes=[f"wb{wbb}"])
            for tt in range(8):
                bank = (mc * 8 + tt) % 2
                for dt in range(16):
                    tp.op("tensor", lambda h, b=bank, d=dt, t=tt, w=wbb: h.matmul(
                        pb[b][:, 0:256],
                        A2[:, d * 1024 + t * 128: d * 1024 + (t + 1) * 128],
                        WB[:, w * 4096 + d * 256: w * 4096 + d * 256 + 256],
                        start=(d == 0), stop=(d == 15)),
                        wait=([ln_marks[half * 8 + tt]] if dt == 0 else []),
                        inc=([VMM] if dt == 15 else []),
                        reads=([f"wb{wbb}"] if dt == 0 else []),
                        writes=([f"pb{bank}"] if dt == 0 else []))
                vm = VMM.mark()
                tp.note_write(f"pb{bank}", vm)
                tp.note_read("A2lo", vm)
                tp.note_read(f"wb{wbb}", vm)
                tp.op("vector", lambda h, b=bank: h.tensor_copy(
                    VST[:, b * 256:(b + 1) * 256], pb[b][:, 0:256]),
                    inc=[VEV], reads=[f"pb{bank}"], writes=[f"vst{bank}"])
                tp.op("sync", lambda h, b=bank, t=tt, m=mc, hf=half: h.dma_start(
                    s_v[128 * (hf * 8 + t):128 * (hf * 8 + t + 1),
                        m * 256:(m + 1) * 256],
                    VST[:, b * 256:(b + 1) * 256]),
                    inc=[SVW], dma=True, reads=[f"vst{bank}"])
        v_done = SVW.mark()
        for qr in range(2):
            qq = half * 2 + qr
            for mc in range(16):
                wbb = mc % 2
                dst = WB[:, wbb * 4096:(wbb + 1) * 4096]
                if mc not in wq_cached:
                    src = wqkvB[mc * 128:(mc + 1) * 128, :]
                    tp.op("gpsimd", lambda h, d=dst, s=src: h.dma_start(d, s),
                          inc=[WLD], dma=True, writes=[f"wb{wbb}"])
                    tp.op("sync", lambda h, d=s_wq[mc * 128:(mc + 1) * 128, :],
                          s=dst: h.dma_start(d, s),
                          inc=[WQW], dma=True, reads=[f"wb{wbb}"])
                    wq_cached[mc] = WQW.mark()
                else:
                    tp.op("sync", lambda h, d=dst,
                          s=s_wq[mc * 128:(mc + 1) * 128, :]: h.dma_start(d, s),
                          wait=[wq_cached[mc]],
                          inc=[WLD], dma=True, writes=[f"wb{wbb}"])
                for mt2 in range(2):
                    mt = 2 * mc + mt2
                    bank = mt % 4
                    for dt in range(16):
                        tp.op("tensor", lambda h, b=bank, d=dt, q=qr, w=wbb, m2=mt2: h.matmul(
                            pb[b][:, 0:512],
                            WB[:, w * 4096 + d * 256 + m2 * 128:
                               w * 4096 + d * 256 + m2 * 128 + 128],
                            A2[:, d * 1024 + q * 512: d * 1024 + (q + 1) * 512],
                            start=(d == 0), stop=(d == 15)),
                            wait=([ln_marks[half * 8 + 4 * qr + 3]] if dt == 0 else []),
                            inc=([QKM] if dt == 15 else []),
                            reads=([f"wb{wbb}"] if dt == 0 else []),
                            writes=([f"pb{bank}"] if dt == 0 else []))
                    qm = QKM.mark()
                    tp.note_write(f"pb{bank}", qm)
                    tp.note_read("A2lo", qm)
                    tp.note_read(f"wb{wbb}", qm)
                    i = mt % 16
                    base = 16384 + (8192 if mt >= 16 else 0) + i
                    dstq = bass.AP(A2, base, [[32768, 128], [128, 64], [16, 8]])
                    srcq = bass.AP(pb[bank], 0, [[512, 128], [8, 64], [1, 8]])
                    tp.op("vector", lambda h, d=dstq, s=srcq: h.tensor_copy(d, s),
                          inc=[QEV], reads=[f"pb{bank}"],
                          writes=["rQI" if mt < 16 else "rKI"])
            qk_done = QEV.mark()
            for tt4 in range(4):
                TTI = 4 * qq + tt4
                vb = tt4 % 2
                for tau in range(8):
                    dstv = bass.AP(VINT, tau * 16 * 4128 + vb * 2064,
                                   [[4128, 16], [129, 16], [1, 128]])
                    srcv = bass.AP(s_v.tensor, (128 * TTI + tau) * 2048,
                                   [[128, 16], [8 * 2048, 16], [1, 128]])
                    tp.op("sync", lambda h, d=dstv, s=srcv: h.dma_start(d, s),
                          wait=([v_done] if tau == 0 else []), inc=[VGL], dma=True,
                          reads=(["vint_ones"] if tau == 0 else []),
                          writes=([f"vint{vb}"] if tau == 0 else []),
                          )
                vload = VGL.mark()
                tp.note_write(f"vint{vb}", vload)
                for sg in range(4):
                    pE = sg % 2
                    for g in range(4):
                        gq = tt4 * 16 + sg * 4 + g
                        tp.op("tensor", lambda h, gq=gq, b=pE, g=g: h.matmul(
                            pb[b][:, g * 128:(g + 1) * 128],
                            A2[:, 24576 + gq * 128: 24576 + (gq + 1) * 128],
                            A2[:, 16384 + gq * 128: 16384 + (gq + 1) * 128],
                            start=True, stop=True),
                            wait=[qk_done],
                            inc=([GQK] if g == 3 else []),
                            reads=(["rQI", "rKI"] if g == 0 else []),
                            writes=([f"pbE{pE}"] if g == 0 else []))
                    qkm = GQK.mark()
                    tp.note_write(f"pbE{pE}", qkm)
                    tp.note_read("rQI", qkm)
                    tp.note_read("rKI", qkm)
                    tp.op("scalar", lambda h, b=pE: h.activation(
                        EBUF[:, b * 512:(b + 1) * 512], pb[b][:, 0:512], AF.Exp,
                        bias=np.float32(-8.0), scale=np.float32(INV_SQRT_HD)),
                        inc=[GEX], reads=[f"pbE{pE}"], writes=[f"eb{pE}"])
                    tp.op("vector", lambda h, b=pE: h.tensor_mul(
                        EBUF[:, b * 512:(b + 1) * 512],
                        EBUF[:, b * 512:(b + 1) * 512], CM4[:, :]),
                        wait=[cld], inc=[GMS], reads=[f"eb{pE}"], writes=[f"eb{pE}"])
                    for g in range(4):
                        gq = tt4 * 16 + sg * 4 + g
                        ob = 2 + g // 2
                        oc = (g % 2) * 129
                        tp.op("tensor", lambda h, b=pE, g=g, gq=gq, v=vb,
                              ob=ob, oc=oc: h.matmul(
                            pb[ob][:, oc:oc + 129],
                            EBUF[:, b * 512 + g * 128: b * 512 + (g + 1) * 128],
                            VINT[:, v * 2064 + (gq % 16) * 129:
                                 v * 2064 + (gq % 16 + 1) * 129],
                            start=True, stop=True),
                            wait=[vload],
                            inc=([GAV] if g == 3 else []),
                            reads=([f"eb{pE}", f"vint{vb}"] if g == 0 else []),
                            writes=(["pbO"] if g == 0 else []))
                    avm = GAV.mark()
                    tp.note_write("pbO", avm)
                    tp.note_read(f"eb{pE}", avm)
                    tp.note_read(f"vint{vb}", avm)
                    for ob in (2, 3):
                        din = bass.AP(pb[ob], 128, [[512, 128], [129, 2]])
                        tp.op("vector", lambda h, d=din, ob=ob: h.reciprocal(
                            RC[:, (ob - 2) * 2:(ob - 2) * 2 + 2], d),
                            inc=[GRC], reads=["pbO"], writes=["rc4"])
                    for g in range(4):
                        ob = 2 + g // 2
                        oc = (g % 2) * 129
                        tp.op("vector", lambda h, b=pE, g=g, ob=ob, oc=oc:
                              h.tensor_scalar(
                                  OBUF[:, b * 512 + g * 128: b * 512 + (g + 1) * 128],
                                  pb[ob][:, oc:oc + 128],
                                  RC[:, g:g + 1], None, ALU.mult),
                              inc=([GTS] if g == 3 else []),
                              reads=(["pbO", "rc4"] if g == 0 else []),
                              writes=([f"ob{pE}"] if g == 0 else []))
                    tsm = GTS.mark()
                    tp.note_write(f"ob{pE}", tsm)
                    tp.note_read("pbO", tsm)
                    tp.note_read("rc4", tsm)
                    for g in range(4):
                        gq = tt4 * 16 + sg * 4 + g
                        Gg = 64 * qq + gq
                        tb = 4 + g // 2
                        tc = (g % 2) * 128
                        tp.op("tensor", lambda h, b=pE, g=g, tb=tb, tc=tc:
                              h.transpose(
                                  pbf[tb][:, tc:tc + 128],
                                  OBUF[:, b * 512 + g * 128: b * 512 + (g + 1) * 128],
                                  CID[:, :]),
                              inc=([GTP] if g == 3 else []),
                              reads=([f"ob{pE}"] if g == 0 else []),
                              writes=(["pbT"] if g == 0 else []))
                    tpm = GTP.mark()
                    tp.note_write("pbT", tpm)
                    tp.note_read(f"ob{pE}", tpm)
                    for g in range(4):
                        gq = tt4 * 16 + sg * 4 + g
                        Gg = 64 * qq + gq
                        tb = 4 + g // 2
                        tc = (g % 2) * 128
                        gg = Gg // 2
                        ctb = 8 * (Gg % 2)
                        dstp = bass.AP(A1, ctb * 2048 + gg,
                                       [[32768, 128], [2048, 8], [128, 16]])
                        srcp = pbf[tb][:, tc:tc + 128].rearrange(
                            "p (t i) -> p t i", t=8)
                        tp.op("vector", lambda h, d=dstp, s=srcp: h.tensor_copy(d, s),
                              inc=([GPT] if g == 3 else []),
                              reads=(["pbT"] if g == 0 else []))
                    ptm = GPT.mark()
                    tp.note_read("pbT", ptm)

    tp.barrier("AB")

    for tile in range(16):
        emit_ln_tile(xb, tile, lambda dt, t: dt * 2048 + t * 128, "B")
    tp.barrier("B0")

    BMA, BMB, SIL, ULC, GTM, ZMM, ZEV, HSW = (
        S("BMA"), S("BMB"), S("SIL"), S("ULC"), S("GTM"), S("ZMM"), S("ZEV"),
        S("HSW"))
    BLD, BFY, BSW = S("BLD"), S("BFY"), S("BSW")

    bfly_q = []

    def queue_butterfly_chunk(bc):
        srcl = w2B[bc * 128:(bc + 1) * 128, :]
        bfly_q.append(lambda s=srcl: tp.op(
            "gpsimd", lambda h, d=XF[:, :], ss=s: h.dma_start(d, ss),
            inc=[BLD], dma=True, writes=["bfA"], reads=["dma:bsw"]))
        for st in range(6):
            s2 = 1 << st
            na = 32 // s2
            inner = s2 * 128
            a_lo = bass.AP(XF, 0, [[8192, 128], [2 * inner, na], [1, inner]])
            a_hi = bass.AP(XF, inner, [[8192, 128], [2 * inner, na], [1, inner]])
            xct = bass.AP(XC, 0, [[4096, 128], [inner, na], [1, inner]])
            bfly_q.append(lambda o=xct, a=a_lo: tp.op(
                "vector", lambda h: h.tensor_copy(o, a),
                inc=[BFY], reads=["bfA"], writes=["xctmp"]))
            bfly_q.append(lambda o=a_lo, a=a_lo, b=a_hi: tp.op(
                "vector", lambda h: h.tensor_add(o, a, b),
                inc=[BFY], reads=["bfA"], writes=["bfA"]))
            bfly_q.append(lambda o=a_hi, a=xct, b=a_hi: tp.op(
                "vector", lambda h: h.tensor_sub(o, a, b),
                inc=[BFY], reads=["xctmp", "bfA"], writes=["bfA"]))
        dsts = s_w2h[bc * 128:(bc + 1) * 128, :]
        bfly_q.append(lambda d=dsts: tp.op(
            "sync", lambda h: h.dma_start(d, XF[:, :]),
            inc=[BSW], dma=True, reads=["bfA"], writes=["dma:bsw"]))

    for _bc in range(16):
        queue_butterfly_chunk(_bc)

    pending_z = []

    def flush_z():
        while pending_z:
            pending_z.pop(0)()

    for ht in range(64):
        hb = ht % 2
        for widx, wt in enumerate((w1B, w3B)):
            src = wt[ht * 128:(ht + 1) * 128, :]
            qoff = hb * 4096 + widx * 2048
            dst = WB[:, qoff:qoff + 2048]
            tp.op("gpsimd", lambda h, d=dst, s=src: h.dma_start(d, s),
                  inc=[WLD], dma=True, writes=[f"wq{hb}{widx}"])
        for rb in range(4):
            flush_z()
            for _ in range(2):
                if bfly_q:
                    bfly_q.pop(0)()
            bA = rb % 2
            bB = 2 + rb % 2
            for widx, (bk, msem) in enumerate(((bA, BMA), (bB, BMB))):
                qoff = hb * 4096 + widx * 2048
                for dt in range(16):
                    tp.op("tensor", lambda h, b=bk, d=dt, qo=qoff, r=rb: h.matmul(
                        pb[b][:, 0:512],
                        WB[:, qo + d * 128: qo + (d + 1) * 128],
                        A2[:, d * 2048 + r * 512: d * 2048 + (r + 1) * 512],
                        start=(d == 0), stop=(d == 15)),
                        inc=([msem] if dt == 15 else []),
                        reads=(["A2", f"wq{hb}{widx}"] if dt == 0 else []),
                        writes=([f"pb{bk}"] if dt == 0 else []))
                bm = msem.mark()
                tp.note_write(f"pb{bk}", bm)
                tp.note_read("A2", bm)
                tp.note_read(f"wq{hb}{widx}", bm)
            tp.op("scalar", lambda h, b=bA, r=rb: h.activation(
                SAB[:, (r % 2) * 512:(r % 2) * 512 + 512], pb[b][:, 0:512],
                AF.Silu), inc=[SIL], reads=[f"pb{bA}"], writes=[f"sab{rb % 2}"])
            tp.op("scalar", lambda h, b=bB, r=rb: h.activation(
                SBB[:, (r % 2) * 512:(r % 2) * 512 + 512], pb[b][:, 0:512],
                AF.Copy, bias=0.0, scale=np.float32(FWHT_SCALE)),
                inc=[ULC], reads=[f"pb{bB}"], writes=[f"sbb{rb % 2}"])
            tp.op("vector", lambda h, r=rb: h.tensor_mul(
                GBUF[:, (r % 2) * 512:(r % 2) * 512 + 512],
                SAB[:, (r % 2) * 512:(r % 2) * 512 + 512],
                SBB[:, (r % 2) * 512:(r % 2) * 512 + 512]),
                inc=[GTM], reads=[f"sab{rb % 2}", f"sbb{rb % 2}"],
                writes=[f"gb{rb % 2}"])
            zb = 4 + rb % 2

            def _zops(r=rb, z=zb, t=ht):
                tp.op("tensor", lambda h: h.matmul(
                    pb[z][:, 0:512], CH[:, :],
                    GBUF[:, (r % 2) * 512:(r % 2) * 512 + 512],
                    start=True, stop=True), wait=[cld],
                    inc=[ZMM], reads=[f"gb{r % 2}"], writes=[f"pb{z}"])
                tp.op("vector", lambda h: h.tensor_copy(
                    HST[:, (r % 2) * 512:(r % 2) * 512 + 512], pb[z][:, 0:512]),
                    inc=[ZEV], reads=[f"pb{z}"], writes=[f"hst{r % 2}"])
                tp.op("scalar", lambda h: h.dma_start(
                    s_hid[128 * t:128 * (t + 1), r * 512:(r + 1) * 512],
                    HST[:, (r % 2) * 512:(r % 2) * 512 + 512]),
                    inc=[HSW], dma=True, reads=[f"hst{r % 2}"], writes=["dma:hsw"])

            pending_z.append(_zops)
            if len(pending_z) > 1:
                pending_z.pop(0)()
    flush_z()
    while bfly_q:
        bfly_q.pop(0)()

    tp.barrier("BC")
    hid_done = HSW.mark()
    w2h_done = BSW.mark()

    HQL, WCL, PJL, XBL, CMM, YEV, YSW = (
        S("HQL"), S("WCL"), S("PJL"), S("XBL"), S("CMM"), S("YEV"), S("YSW"))
    for q in range(4):
        srch = s_hid.rearrange("(ht p) r -> p ht r", p=128)[
            :, :, q * 512:(q + 1) * 512]
        dsth = A2[:, :].rearrange("p (ht r) -> p ht r", ht=64)
        tp.op("sync", lambda h, d=dsth, s=srch: h.dma_start(d, s),
              wait=[hid_done], inc=[HQL], dma=True, writes=["A2"])
        for dct in range(16):
            wb2 = dct % 2
            wcol = WB[:, :] if wb2 == 0 else XF[:, :]
            wcol_res = "wb_c" if wb2 == 0 else "xf_c"
            srcw = s_w2h[dct * 128:(dct + 1) * 128, :]
            tp.op("sync", lambda h, d=wcol, s=srcw: h.dma_start(d, s),
                  wait=[w2h_done], inc=[WCL], dma=True, writes=[wcol_res])
            srcp = wprojB[dct * 128:(dct + 1) * 128, :]
            dstp = XC[:, wb2 * 2048:(wb2 + 1) * 2048]
            tp.op("gpsimd", lambda h, d=dstp, s=srcp: h.dma_start(d, s),
                  inc=[PJL], dma=True, writes=[f"xc{wb2}"])
            tp.op("scalar", lambda h, d=dct, qi=q, w=wb2: h.dma_start(
                XBT[:, w * 512:(w + 1) * 512],
                xbt[128 * d:128 * (d + 1), qi * 512:(qi + 1) * 512]),
                inc=[XBL], dma=True, writes=[f"xbt{wb2}"])
            bko = dct % 2
            for ct in range(16):
                tp.op("tensor", lambda h, c=ct, w=wb2, qi=q, b=bko: h.matmul(
                    pb[b][:, 0:512],
                    XC[:, w * 2048 + c * 128: w * 2048 + (c + 1) * 128],
                    A1[:, c * 2048 + qi * 512: c * 2048 + qi * 512 + 512],
                    start=(c == 0), stop=False),
                    reads=([f"xc{wb2}", wcol_res, "A2"] if ct == 0 else []),
                    writes=([f"pb{bko}"] if ct == 0 else []))
            for C in range(64):
                tp.op("tensor", lambda h, c=C, w=wcol, b=bko: h.matmul(
                    pb[b][:, 0:512],
                    w[:, c * 128:(c + 1) * 128],
                    A2[:, c * 512:(c + 1) * 512],
                    start=False, stop=(c == 63)),
                    inc=([CMM] if C == 63 else []))
            cm2 = CMM.mark()
            tp.note_write(f"pb{bko}", cm2)
            tp.note_read(f"xc{wb2}", cm2)
            tp.note_read(wcol_res, cm2)
            tp.note_read("A2", cm2)
            tp.op("vector", lambda h, b=bko, w=wb2: h.tensor_add(
                YST[:, w * 512:(w + 1) * 512], pb[b][:, 0:512],
                XBT[:, w * 512:(w + 1) * 512]),
                inc=[YEV], reads=[f"pb{bko}", f"xbt{wb2}"], writes=[f"yst{wb2}"])
            tp.op("scalar", lambda h, d=dct, qi=q, w=wb2: h.dma_start(
                yT[128 * d:128 * (d + 1), qi * 512:(qi + 1) * 512],
                YST[:, w * 512:(w + 1) * 512]),
                inc=[YSW], dma=True, reads=[f"yst{wb2}"])
    tp.raw_wait("sync", YSW, YSW.n)

    with nc.Block() as block:
        tp.replay(block)
    ctx.close()
    return nc


_NC_CACHE = {}


def _get_program():
    if "nc" not in _NC_CACHE:
        _NC_CACHE["nc"] = build_program()
    return _NC_CACHE["nc"]


def _consts():
    H = np.array([[1.0]], np.float32)
    while H.shape[0] < 128:
        H = np.block([[H, H], [H, -H]])
    cid = np.eye(128, dtype=BF)
    cm = np.zeros((128, 128), dtype=BF)
    for t in range(8):
        cm[16 * t:16 * t + 16, 16 * t:16 * t + 16] = 1.0
    return cid, H.astype(BF), cm


def kernel(L0_attn_x, L0_attn_w_qkv, L0_attn_w_proj, L0_ffn_W1, L0_ffn_W2,
           L0_ffn_W3):
    X = np.ascontiguousarray(np.asarray(L0_attn_x, dtype=np.float32)[0])
    wqkvT = np.asarray(L0_attn_w_qkv, np.float32).T
    wprojT = np.asarray(L0_attn_w_proj, np.float32).T
    w1T = np.asarray(L0_ffn_W1, np.float32).T
    w3T = np.asarray(L0_ffn_W3, np.float32).T
    w2T = np.asarray(L0_ffn_W2, np.float32).T
    wqkvB = np.ascontiguousarray(
        wqkvT.reshape(16, 128, 24, 256).transpose(2, 1, 0, 3)).reshape(24 * 128, 4096)
    wprojB = np.ascontiguousarray(
        wprojT.reshape(16, 128, 16, 128).transpose(2, 1, 0, 3)).reshape(16 * 128, 2048)
    w1B = np.ascontiguousarray(
        w1T.reshape(16, 128, 64, 128).transpose(2, 1, 0, 3)).reshape(64 * 128, 2048)
    w3B = np.ascontiguousarray(
        w3T.reshape(16, 128, 64, 128).transpose(2, 1, 0, 3)).reshape(64 * 128, 2048)
    w2B = np.ascontiguousarray(
        w2T.reshape(64, 128, 16, 128).transpose(2, 1, 0, 3)).reshape(16 * 128, 8192)
    cid, ch, cm = _consts()
    cm4 = np.ascontiguousarray(np.tile(np.asarray(cm), (1, 4)))

    in_maps = []
    s_list = []
    for c in range(8):
        Sidx = np.array([h * 1024 + 128 * c + gg for h in range(16)
                         for gg in range(128)])
        s_list.append(Sidx)
        xbv = np.ascontiguousarray(X[Sidx])
        in_maps.append(dict(
            xa=np.ascontiguousarray(X[2048 * c:2048 * (c + 1)]),
            xb=xbv,
            xbt=np.ascontiguousarray(xbv.T),
            wqkvB=wqkvB, wprojB=wprojB, w1B=w1B, w3B=w3B, w2B=w2B,
            c_id=cid, c_h=ch, c_m=cm, c_m4=cm4,
        ))

    nc = _get_program()
    res = run_bass_kernel_spmd(nc, in_maps, core_ids=list(range(8)))
    out = np.empty((16384, 2048), np.float32)
    for c in range(8):
        out[s_list[c]] = res.results[c]["yT"].T
    return out.reshape(1, 16384, 2048)


# version 06
# speedup vs baseline: 1.5947x; 1.1619x over previous
import contextlib
import numpy as np
import ml_dtypes
import concourse.bass as bass
import concourse.mybir as mybir
from concourse.bass_utils import run_bass_kernel_spmd

F32 = mybir.dt.float32
BF16 = mybir.dt.bfloat16
AF = mybir.ActivationFunctionType
ALU = mybir.AluOpType
BF = ml_dtypes.bfloat16

ENGS = ("tensor", "vector", "scalar", "gpsimd", "sync")
D = 2048
NT = 2048
MQKV = 6144
HF = 8192
INV_SQRT_HD = float(1.0 / np.sqrt(128.0))
FWHT_SCALE = float(1.0 / np.sqrt(8192.0))
LN_EPS = 1e-5


class Sem:
    __slots__ = ("h", "n")

    def __init__(self, h):
        self.h = h
        self.n = 0

    def mark(self):
        return (self, self.n)


class Tape:

    def __init__(self, nc, ctx):
        self.nc = nc
        self.tape = {e: [] for e in ENGS}
        self._ctx = ctx
        self._sems = {}
        self.wr = {}
        self.rd = {}

    def sem(self, name):
        if name not in self._sems:
            self._sems[name] = Sem(self._ctx.enter_context(self.nc.semaphore(name)))
        return self._sems[name]

    def op(self, eng, fn, wait=(), inc=(), dma=False, reads=(), writes=()):
        waits = []
        for w in wait:
            if isinstance(w, Sem):
                waits.append((w, w.n))
            else:
                waits.append((w[0], w[1]))
        for r in reads:
            m = self.wr.get(r)
            if m is not None:
                waits.append(m)
        for r in writes:
            m = self.wr.get(r)
            if m is not None:
                waits.append(m)
            for mm in self.rd.get(r, ()):
                waits.append(mm)
        byname = {}
        for s, v in waits:
            if v > 0 and (s not in byname or v > byname[s]):
                byname[s] = v
        waits = [(s, v) for s, v in byname.items()]
        incs = []
        for i in inc:
            if isinstance(i, Sem):
                incs.append((i, 16 if dma else 1))
            else:
                incs.append((i[0], i[1]))
        for s, k in incs:
            s.n += k
        self.tape[eng].append((waits, fn, incs))
        mark = (incs[-1][0], incs[-1][0].n) if incs else None
        if mark is not None:
            for r in writes:
                self.wr[r] = mark
                self.rd[r] = []
            for r in reads:
                self.rd.setdefault(r, []).append(mark)

    def note_read(self, r, mark):
        self.rd.setdefault(r, []).append(mark)

    def note_write(self, r, mark):
        self.wr[r] = mark
        self.rd[r] = []

    def raw_wait(self, eng, sem, val):
        self.tape[eng].append(([(sem, val)], None, []))

    def barrier(self, name):
        b = self.sem("bar_" + name)
        target = b.n + 5
        for e in ENGS:
            self.tape[e].append(([], lambda h: h.nop(), [(b, 1)]))
        b.n = target
        for e in ENGS:
            self.tape[e].append(([(b, target)], None, []))
        self.wr = {k: v for k, v in self.wr.items() if k.startswith("dma:")}
        self.rd = {}

    def replay(self, block):
        def mk(eng_name):
            def body(h):
                for waits, fn, incs in self.tape[eng_name]:
                    for s, v in waits:
                        h.wait_ge(s.h, v)
                    if fn is None:
                        continue
                    instr = fn(h)
                    for s, k in incs:
                        instr.then_inc(s.h, k)
            return body

        block.tensor(mk("tensor"))
        block.vector(mk("vector"))
        block.scalar(mk("scalar"))
        block.gpsimd(mk("gpsimd"))
        block.sync(mk("sync"))


def build_program():
    nc = bass.Bass("TRN2", target_bir_lowering=False)

    xa = nc.dram_tensor("xa", [NT, D], F32, kind="ExternalInput").ap()
    xb = nc.dram_tensor("xb", [NT, D], F32, kind="ExternalInput").ap()
    xbt = nc.dram_tensor("xbt", [D, NT], F32, kind="ExternalInput").ap()
    wqkvB = nc.dram_tensor("wqkvB", [24 * 128, 4096], F32, kind="ExternalInput").ap()
    wprojB = nc.dram_tensor("wprojB", [16 * 128, 2048], F32, kind="ExternalInput").ap()
    w1B = nc.dram_tensor("w1B", [64 * 128, 2048], F32, kind="ExternalInput").ap()
    w3B = nc.dram_tensor("w3B", [64 * 128, 2048], F32, kind="ExternalInput").ap()
    w2B = nc.dram_tensor("w2B", [16 * 128, 8192], F32, kind="ExternalInput").ap()
    c_id = nc.dram_tensor("c_id", [128, 128], BF16, kind="ExternalInput").ap()
    c_h = nc.dram_tensor("c_h", [128, 128], BF16, kind="ExternalInput").ap()
    c_m = nc.dram_tensor("c_m", [128, 128], BF16, kind="ExternalInput").ap()
    c_m4 = nc.dram_tensor("c_m4", [128, 512], BF16, kind="ExternalInput").ap()
    yT = nc.dram_tensor("yT", [D, NT], F32, kind="ExternalOutput").ap()
    s_v = nc.dram_tensor("s_v", [NT, D], BF16, kind="Internal").ap()
    s_wq = nc.dram_tensor("s_wq", [24 * 128, 4096], BF16, kind="Internal").ap()
    s_hid = nc.dram_tensor("s_hid", [HF, NT], BF16, kind="Internal").ap()
    s_w2h = nc.dram_tensor("s_w2h", [D, HF], BF16, kind="Internal").ap()

    ctx = contextlib.ExitStack()
    sb = lambda name, shape, dt: ctx.enter_context(nc.sbuf_tensor(name, shape, dt))

    A1 = sb("A1", [128, 32768], BF16)
    A2 = sb("A2", [128, 32768], BF16)
    WB = sb("WB", [128, 8192], BF16)
    XF = sb("XF", [128, 8192], BF16)
    XC = sb("XC", [128, 4096], BF16)
    VINT = sb("VINT", [128, 4128], BF16)
    VST = sb("VST", [128, 512], BF16)
    SAB = sb("SAB", [128, 2048], BF16)
    SBB = sb("SBB", [128, 2048], BF16)
    GBUF = sb("GBUF", [128, 2048], BF16)
    HST = sb("HST", [128, 1024], BF16)
    YST = sb("YST", [128, 1024], F32)
    XBT = sb("XBT", [128, 1024], F32)
    EBUF = sb("EBUF", [128, 1024], BF16)
    OBUF = sb("OBUF", [128, 1024], BF16)
    RC = sb("RC", [128, 4], F32)
    SUMS = sb("SUMS", [128, 16], F32)
    SQS = sb("SQS", [128, 16], F32)
    MNEG = sb("MNEG", [128, 16], F32)
    RSTD = sb("RSTD", [128, 16], F32)
    TMPA = sb("TMPA", [128, 16], F32)
    TMPB = sb("TMPB", [128, 16], F32)
    CID = sb("CID", [128, 128], BF16)
    CH = sb("CH", [128, 128], BF16)
    CM = sb("CM", [128, 128], BF16)
    CM4 = sb("CM4", [128, 512], BF16)

    pb = [ctx.enter_context(nc.psum_tensor(f"pb{i}", [128, 512], F32)) for i in range(8)]
    DUMP = YST[:, :].bitcast(BF16)
    pbf = [p[:, :].bitcast(BF16) for p in pb]
    XFf32 = XF[:, :].bitcast(F32)

    tp = Tape(nc, ctx)
    S = tp.sem

    def mm_nold(h, out, lhsT, rhs, start, stop):
        inst = mybir.InstMatmult(
            name=nc.get_next_instruction_name(),
            ins=[h.lower_ap(rhs), h.lower_ap(lhsT, for_matmul_weights=True)],
            outs=[h.lower_ap(out)],
            replication_resolution=0, replication_shift_amnt=0,
            replication_num_rows=0,
            start_tensor_calc=start, stop_tensor_calc=stop,
            tile_size=(128, 128), tile_position=(0, 0), ldweights=False,
            bass_skip_group_check=True)
        return h.add_instruction(inst)
    CLD = S("CLD")

    for dst, src in ((CID, c_id), (CH, c_h), (CM, c_m), (CM4, c_m4)):
        tp.op("sync", lambda h, d=dst, s=src: h.dma_start(d[:, :], s), inc=[CLD], dma=True)
    cld = CLD.mark()
    ones_ap = bass.AP(VINT, 128, [[4128, 128], [129, 32], [1, 1]])
    tp.op("vector", lambda h: h.memset(ones_ap, 1.0), inc=[S("MISC")], writes=["vint_ones"])

    def emit_ln_tile(xsrc, tile, xn_col_base, P):
        b = tile % 2
        xld = XFf32[:, b * 2048:(b + 1) * 2048]
        xcb = XC[:, b * 2048:b * 2048 + 2048]
        col = tile % 16
        XL, XR, ST, VS, TT, TE = (S(P + "XL"), S(P + "XR"), S(P + "ST"),
                                  S(P + "VS"), S(P + "TT"), S(P + "TE"))
        tp.op("scalar", lambda h, x=xld, t=tile: h.dma_start(
            x, xsrc[128 * t:128 * (t + 1), :]), inc=[XL], dma=True,
            writes=[f"xf{b}"])
        tp.op("scalar", lambda h, x=xld, c=col: h.activation(
            DUMP[:, :], x, AF.Copy, bias=0.0, scale=np.float32(1.0),
            accum_out=SUMS[:, c:c + 1]), inc=[ST],
            reads=[f"xf{b}"], writes=[f"sum{col}"])
        tp.op("scalar", lambda h, x=xld, c=col: h.activation(
            DUMP[:, :], x, AF.Square, accum_out=SQS[:, c:c + 1]), inc=[ST],
            reads=[f"xf{b}"], writes=[f"sq{col}"])
        tp.op("vector", lambda h, c=col: h.tensor_scalar(
            MNEG[:, c:c + 1], SUMS[:, c:c + 1], float(-1.0 / 2048.0), None,
            ALU.mult), inc=[VS], reads=[f"sum{col}"], writes=[f"mneg{col}"])
        tp.op("vector", lambda h, c=col: h.tensor_scalar(
            TMPA[:, c:c + 1], SQS[:, c:c + 1], float(1.0 / 2048.0), None,
            ALU.mult), inc=[VS], reads=[f"sq{col}"], writes=[f"ta{col}"])
        tp.op("vector", lambda h, c=col: h.tensor_mul(
            TMPB[:, c:c + 1], MNEG[:, c:c + 1], MNEG[:, c:c + 1]), inc=[VS],
            reads=[f"mneg{col}"], writes=[f"tb{col}"])
        tp.op("vector", lambda h, c=col: h.tensor_sub(
            TMPA[:, c:c + 1], TMPA[:, c:c + 1], TMPB[:, c:c + 1]), inc=[VS],
            reads=[f"tb{col}"], writes=[f"ta{col}"])
        tp.op("scalar", lambda h, c=col: h.activation(
            TMPB[:, c:c + 1], TMPA[:, c:c + 1], AF.Ln, bias=np.float32(LN_EPS)),
            inc=[ST], reads=[f"ta{col}"], writes=[f"tb{col}"])
        tp.op("scalar", lambda h, c=col: h.activation(
            RSTD[:, c:c + 1], TMPB[:, c:c + 1], AF.Exp, scale=np.float32(-0.5)),
            inc=[ST], reads=[f"tb{col}"], writes=[f"rstd{col}"])
        tp.op("vector", lambda h, x=xld, xc=xcb, c=col: h.tensor_scalar(
            xc, x, MNEG[:, c:c + 1], RSTD[:, c:c + 1], ALU.add, ALU.mult),
            inc=[XR], reads=[f"xf{b}", f"mneg{col}", f"rstd{col}"],
            writes=[f"xc{b}"])
        for dt in range(16):
            bank = 6 + dt % 2
            tp.op("tensor", lambda h, xc=xcb, d=dt, bk=bank: h.transpose(
                pbf[bk][:, 0:128], xc[:, d * 128:(d + 1) * 128], CID[:, :]),
                wait=[cld], inc=[TT], reads=[f"xc{b}"], writes=[f"pb{bank}"])
            dcol = xn_col_base(dt, tile)
            tp.op("vector", lambda h, bk=bank, c=dcol: h.tensor_copy(
                A2[:, c:c + 128], pbf[bk][:, 0:128]), inc=[TE],
                reads=[f"pb{bank}"], writes=["A2lo" if P == "A" else "A2"])
        return TE.mark()

    WLD, VMM, VEV, SVW = S("WLD"), S("VMM"), S("VEV"), S("SVW")
    WQW = S("WQW")
    wq_cached = {}
    QKM, QEV, VGL = S("QKM"), S("QEV"), S("VGL")
    GQK, GEX, GMS, GAV, GRC, GTS, GTP, GPT = (
        S("GQK"), S("GEX"), S("GMS"), S("GAV"), S("GRC"), S("GTS"), S("GTP"), S("GPT"))

    ln_marks = []
    for half in range(2):
        for t8 in range(8):
            ln_marks.append(emit_ln_tile(xa, half * 8 + t8,
                            lambda dt, t: dt * 1024 + (t % 8) * 128, "A"))
        for mc in range(8):
            wbb = mc % 2
            blk = 16 + mc
            dst = WB[:, wbb * 4096:(wbb + 1) * 4096]
            if blk not in wq_cached:
                src = wqkvB[blk * 128:(blk + 1) * 128, :]
                tp.op("gpsimd", lambda h, d=dst, s=src: h.dma_start(d, s),
                      inc=[WLD], dma=True, writes=[f"wb{wbb}"])
                tp.op("sync", lambda h, d=s_wq[blk * 128:(blk + 1) * 128, :],
                      s=dst: h.dma_start(d, s),
                      inc=[WQW], dma=True, reads=[f"wb{wbb}"])
                wq_cached[blk] = WQW.mark()
            else:
                tp.op("sync", lambda h, d=dst,
                      s=s_wq[blk * 128:(blk + 1) * 128, :]: h.dma_start(d, s),
                      wait=[wq_cached[blk]],
                      inc=[WLD], dma=True, writes=[f"wb{wbb}"])
            for tt in range(8):
                bank = (mc * 8 + tt) % 2
                for dt in range(16):
                    tp.op("tensor", lambda h, b=bank, d=dt, t=tt, w=wbb: h.matmul(
                        pb[b][:, 0:256],
                        A2[:, d * 1024 + t * 128: d * 1024 + (t + 1) * 128],
                        WB[:, w * 4096 + d * 256: w * 4096 + d * 256 + 256],
                        start=(d == 0), stop=(d == 15)),
                        wait=([ln_marks[half * 8 + tt]] if dt == 0 else []),
                        inc=([VMM] if dt == 15 else []),
                        reads=([f"wb{wbb}"] if dt == 0 else []),
                        writes=([f"pb{bank}"] if dt == 0 else []))
                vm = VMM.mark()
                tp.note_write(f"pb{bank}", vm)
                tp.note_read("A2lo", vm)
                tp.note_read(f"wb{wbb}", vm)
                tp.op("vector", lambda h, b=bank: h.tensor_copy(
                    VST[:, b * 256:(b + 1) * 256], pb[b][:, 0:256]),
                    inc=[VEV], reads=[f"pb{bank}"], writes=[f"vst{bank}"])
                tp.op("sync", lambda h, b=bank, t=tt, m=mc, hf=half: h.dma_start(
                    s_v[128 * (hf * 8 + t):128 * (hf * 8 + t + 1),
                        m * 256:(m + 1) * 256],
                    VST[:, b * 256:(b + 1) * 256]),
                    inc=[SVW], dma=True, reads=[f"vst{bank}"])
        v_done = SVW.mark()
        for qr in range(2):
            qq = half * 2 + qr
            for mc in range(16):
                wbb = mc % 2
                dst = WB[:, wbb * 4096:(wbb + 1) * 4096]
                if mc not in wq_cached:
                    src = wqkvB[mc * 128:(mc + 1) * 128, :]
                    tp.op("gpsimd", lambda h, d=dst, s=src: h.dma_start(d, s),
                          inc=[WLD], dma=True, writes=[f"wb{wbb}"])
                    tp.op("sync", lambda h, d=s_wq[mc * 128:(mc + 1) * 128, :],
                          s=dst: h.dma_start(d, s),
                          inc=[WQW], dma=True, reads=[f"wb{wbb}"])
                    wq_cached[mc] = WQW.mark()
                else:
                    tp.op("sync", lambda h, d=dst,
                          s=s_wq[mc * 128:(mc + 1) * 128, :]: h.dma_start(d, s),
                          wait=[wq_cached[mc]],
                          inc=[WLD], dma=True, writes=[f"wb{wbb}"])
                for mt2 in range(2):
                    mt = 2 * mc + mt2
                    bank = mt % 4
                    for dt in range(16):
                        tp.op("tensor", lambda h, b=bank, d=dt, q=qr, w=wbb, m2=mt2: h.matmul(
                            pb[b][:, 0:512],
                            WB[:, w * 4096 + d * 256 + m2 * 128:
                               w * 4096 + d * 256 + m2 * 128 + 128],
                            A2[:, d * 1024 + q * 512: d * 1024 + (q + 1) * 512],
                            start=(d == 0), stop=(d == 15)),
                            wait=([ln_marks[half * 8 + 4 * qr + 3]] if dt == 0 else []),
                            inc=([QKM] if dt == 15 else []),
                            reads=([f"wb{wbb}"] if dt == 0 else []),
                            writes=([f"pb{bank}"] if dt == 0 else []))
                    qm = QKM.mark()
                    tp.note_write(f"pb{bank}", qm)
                    tp.note_read("A2lo", qm)
                    tp.note_read(f"wb{wbb}", qm)
                    i = mt % 16
                    base = 16384 + (8192 if mt >= 16 else 0) + i
                    dstq = bass.AP(A2, base, [[32768, 128], [128, 64], [16, 8]])
                    srcq = bass.AP(pb[bank], 0, [[512, 128], [8, 64], [1, 8]])
                    tp.op("vector", lambda h, d=dstq, s=srcq: h.tensor_copy(d, s),
                          inc=[QEV], reads=[f"pb{bank}"],
                          writes=["rQI" if mt < 16 else "rKI"])
            qk_done = QEV.mark()
            for tt4 in range(4):
                TTI = 4 * qq + tt4
                vb = tt4 % 2
                for tau in range(8):
                    dstv = bass.AP(VINT, tau * 16 * 4128 + vb * 2064,
                                   [[4128, 16], [129, 16], [1, 128]])
                    srcv = bass.AP(s_v.tensor, (128 * TTI + tau) * 2048,
                                   [[128, 16], [8 * 2048, 16], [1, 128]])
                    tp.op("sync", lambda h, d=dstv, s=srcv: h.dma_start(d, s),
                          wait=([v_done] if tau == 0 else []), inc=[VGL], dma=True,
                          reads=(["vint_ones"] if tau == 0 else []),
                          writes=([f"vint{vb}"] if tau == 0 else []),
                          )
                vload = VGL.mark()
                tp.note_write(f"vint{vb}", vload)
                for sg in range(4):
                    pE = sg % 2
                    for g in range(4):
                        gq = tt4 * 16 + sg * 4 + g
                        tp.op("tensor", lambda h, gq=gq, b=pE, g=g: h.matmul(
                            pb[b][:, g * 128:(g + 1) * 128],
                            A2[:, 24576 + gq * 128: 24576 + (gq + 1) * 128],
                            A2[:, 16384 + gq * 128: 16384 + (gq + 1) * 128],
                            start=True, stop=True),
                            wait=[qk_done],
                            inc=([GQK] if g == 3 else []),
                            reads=(["rQI", "rKI"] if g == 0 else []),
                            writes=([f"pbE{pE}"] if g == 0 else []))
                    qkm = GQK.mark()
                    tp.note_write(f"pbE{pE}", qkm)
                    tp.note_read("rQI", qkm)
                    tp.note_read("rKI", qkm)
                    tp.op("scalar", lambda h, b=pE: h.activation(
                        EBUF[:, b * 512:(b + 1) * 512], pb[b][:, 0:512], AF.Exp,
                        bias=np.float32(-8.0), scale=np.float32(INV_SQRT_HD)),
                        inc=[GEX], reads=[f"pbE{pE}"], writes=[f"eb{pE}"])
                    tp.op("vector", lambda h, b=pE: h.tensor_mul(
                        EBUF[:, b * 512:(b + 1) * 512],
                        EBUF[:, b * 512:(b + 1) * 512], CM4[:, :]),
                        wait=[cld], inc=[GMS], reads=[f"eb{pE}"], writes=[f"eb{pE}"])
                    for g in range(4):
                        gq = tt4 * 16 + sg * 4 + g
                        ob = 2 + g // 2
                        oc = (g % 2) * 129
                        tp.op("tensor", lambda h, b=pE, g=g, gq=gq, v=vb,
                              ob=ob, oc=oc: h.matmul(
                            pb[ob][:, oc:oc + 129],
                            EBUF[:, b * 512 + g * 128: b * 512 + (g + 1) * 128],
                            VINT[:, v * 2064 + (gq % 16) * 129:
                                 v * 2064 + (gq % 16 + 1) * 129],
                            start=True, stop=True),
                            wait=[vload],
                            inc=([GAV] if g == 3 else []),
                            reads=([f"eb{pE}", f"vint{vb}"] if g == 0 else []),
                            writes=(["pbO"] if g == 0 else []))
                    avm = GAV.mark()
                    tp.note_write("pbO", avm)
                    tp.note_read(f"eb{pE}", avm)
                    tp.note_read(f"vint{vb}", avm)
                    for ob in (2, 3):
                        din = bass.AP(pb[ob], 128, [[512, 128], [129, 2]])
                        tp.op("vector", lambda h, d=din, ob=ob: h.reciprocal(
                            RC[:, (ob - 2) * 2:(ob - 2) * 2 + 2], d),
                            inc=[GRC], reads=["pbO"], writes=["rc4"])
                    for g in range(4):
                        ob = 2 + g // 2
                        oc = (g % 2) * 129
                        tp.op("vector", lambda h, b=pE, g=g, ob=ob, oc=oc:
                              h.tensor_scalar(
                                  OBUF[:, b * 512 + g * 128: b * 512 + (g + 1) * 128],
                                  pb[ob][:, oc:oc + 128],
                                  RC[:, g:g + 1], None, ALU.mult),
                              inc=([GTS] if g == 3 else []),
                              reads=(["pbO", "rc4"] if g == 0 else []),
                              writes=([f"ob{pE}"] if g == 0 else []))
                    tsm = GTS.mark()
                    tp.note_write(f"ob{pE}", tsm)
                    tp.note_read("pbO", tsm)
                    tp.note_read("rc4", tsm)
                    for g in range(4):
                        gq = tt4 * 16 + sg * 4 + g
                        Gg = 64 * qq + gq
                        tb = 4 + g // 2
                        tc = (g % 2) * 128
                        tp.op("tensor", lambda h, b=pE, g=g, tb=tb, tc=tc:
                              h.transpose(
                                  pbf[tb][:, tc:tc + 128],
                                  OBUF[:, b * 512 + g * 128: b * 512 + (g + 1) * 128],
                                  CID[:, :]),
                              inc=([GTP] if g == 3 else []),
                              reads=([f"ob{pE}"] if g == 0 else []),
                              writes=(["pbT"] if g == 0 else []))
                    tpm = GTP.mark()
                    tp.note_write("pbT", tpm)
                    tp.note_read(f"ob{pE}", tpm)
                    for g in range(4):
                        gq = tt4 * 16 + sg * 4 + g
                        Gg = 64 * qq + gq
                        tb = 4 + g // 2
                        tc = (g % 2) * 128
                        gg = Gg // 2
                        ctb = 8 * (Gg % 2)
                        dstp = bass.AP(A1, ctb * 2048 + gg,
                                       [[32768, 128], [2048, 8], [128, 16]])
                        srcp = pbf[tb][:, tc:tc + 128].rearrange(
                            "p (t i) -> p t i", t=8)
                        tp.op("vector", lambda h, d=dstp, s=srcp: h.tensor_copy(d, s),
                              inc=([GPT] if g == 3 else []),
                              reads=(["pbT"] if g == 0 else []))
                    ptm = GPT.mark()
                    tp.note_read("pbT", ptm)

    tp.barrier("AB")

    for tile in range(16):
        emit_ln_tile(xb, tile, lambda dt, t: dt * 2048 + t * 128, "B")
    tp.barrier("B0")

    BMA, BMB, SIL, ULC, GTM, ZMM, ZEV, HSW = (
        S("BMA"), S("BMB"), S("SIL"), S("ULC"), S("GTM"), S("ZMM"), S("ZEV"),
        S("HSW"))
    BLD, BFY, BSW = S("BLD"), S("BFY"), S("BSW")

    bfly_q = []

    def queue_butterfly_chunk(bc):
        srcl = w2B[bc * 128:(bc + 1) * 128, :]
        bfly_q.append(lambda s=srcl: tp.op(
            "gpsimd", lambda h, d=XF[:, :], ss=s: h.dma_start(d, ss),
            inc=[BLD], dma=True, writes=["bfA"], reads=["dma:bsw"]))
        for st in range(6):
            s2 = 1 << st
            na = 32 // s2
            inner = s2 * 128
            a_lo = bass.AP(XF, 0, [[8192, 128], [2 * inner, na], [1, inner]])
            a_hi = bass.AP(XF, inner, [[8192, 128], [2 * inner, na], [1, inner]])
            xct = bass.AP(XC, 0, [[4096, 128], [inner, na], [1, inner]])
            bfly_q.append(lambda o=xct, a=a_lo: tp.op(
                "vector", lambda h: h.tensor_copy(o, a),
                inc=[BFY], reads=["bfA"], writes=["xctmp"]))
            bfly_q.append(lambda o=a_lo, a=a_lo, b=a_hi: tp.op(
                "vector", lambda h: h.tensor_add(o, a, b),
                inc=[BFY], reads=["bfA"], writes=["bfA"]))
            bfly_q.append(lambda o=a_hi, a=xct, b=a_hi: tp.op(
                "vector", lambda h: h.tensor_sub(o, a, b),
                inc=[BFY], reads=["xctmp", "bfA"], writes=["bfA"]))
        dsts = s_w2h[bc * 128:(bc + 1) * 128, :]
        bfly_q.append(lambda d=dsts: tp.op(
            "sync", lambda h: h.dma_start(d, XF[:, :]),
            inc=[BSW], dma=True, reads=["bfA"], writes=["dma:bsw"]))

    for _bc in range(16):
        queue_butterfly_chunk(_bc)

    pending_z = []

    def flush_z():
        while pending_z:
            pending_z.pop(0)()

    for ht in range(64):
        hb = ht % 2
        for widx, wt in enumerate((w1B, w3B)):
            srcw = wt[ht * 128:(ht + 1) * 128, :]
            qoff = hb * 4096 + widx * 2048
            tp.op("gpsimd", lambda h, d=WB[:, qoff:qoff + 2048], s=srcw:
                  h.dma_start(d, s),
                  inc=[WLD], dma=True, writes=[f"wq{hb}{widx}"])
        wld_m = WLD.mark()
        flush_z()
        for _ in range(3):
            if bfly_q:
                bfly_q.pop(0)()
        for widx, msem in ((0, BMA), (1, BMB)):
            qoff = hb * 4096 + widx * 2048
            bb = 0 if widx == 0 else 4
            for dt in range(16):
                for rb in range(4):
                    bk = bb + rb
                    if rb == 0:
                        tp.op("tensor", lambda h, d=dt, qo=qoff, b=bk, r=rb:
                              h.matmul(
                                  pb[b][:, 0:512],
                                  WB[:, qo + d * 128: qo + (d + 1) * 128],
                                  A2[:, d * 2048 + r * 512:
                                     d * 2048 + (r + 1) * 512],
                                  start=(d == 0), stop=(d == 15),
                                  skip_group_check=True),
                              wait=([wld_m] if dt == 0 else []),
                              inc=([msem] if dt == 15 else []),
                              reads=(["A2", f"wq{hb}{widx}"] if dt == 0 else []),
                              writes=([f"pb{bk}"] if dt == 0 else []))
                    else:
                        tp.op("tensor", lambda h, d=dt, qo=qoff, b=bk, r=rb:
                              mm_nold(h, pb[b][:, 0:512],
                                      WB[:, qo + d * 128: qo + (d + 1) * 128],
                                      A2[:, d * 2048 + r * 512:
                                         d * 2048 + (r + 1) * 512],
                                      d == 0, d == 15),
                              inc=([msem] if dt == 15 else []),
                              writes=([f"pb{bk}"] if dt == 0 else []))
            base = msem.n - 4
            for rb in range(4):
                tp.note_write(f"pb{bb + rb}", (msem, base + rb + 1))
            tp.note_read("A2", (msem, msem.n))
            tp.note_read(f"wq{hb}{widx}", (msem, msem.n))
            if widx == 0:
                for rb in range(4):
                    tp.op("scalar", lambda h, r=rb: h.activation(
                        SAB[:, r * 512:(r + 1) * 512], pb[r][:, 0:512],
                        AF.Silu), inc=[SIL], reads=[f"pb{rb}"],
                        writes=[f"sab{rb}"])
        for rb in range(4):
            tp.op("scalar", lambda h, r=rb: h.activation(
                SBB[:, r * 512:(r + 1) * 512], pb[4 + r][:, 0:512],
                AF.Copy, bias=0.0, scale=np.float32(FWHT_SCALE)),
                inc=[ULC], reads=[f"pb{4 + rb}"], writes=[f"sbb{rb}"])
        for _ in range(2):
            if bfly_q:
                bfly_q.pop(0)()
        for rb in range(4):
            tp.op("vector", lambda h, r=rb: h.tensor_mul(
                GBUF[:, r * 512:(r + 1) * 512],
                SAB[:, r * 512:(r + 1) * 512],
                SBB[:, r * 512:(r + 1) * 512]),
                inc=[GTM], reads=[f"sab{rb}", f"sbb{rb}"], writes=[f"gb{rb}"])

        def _zbatch(t=ht):
            for rb in range(4):
                z = 4 + rb
                if rb == 0:
                    tp.op("tensor", lambda h, r=rb, zz=z: h.matmul(
                        pb[zz][:, 0:512], CH[:, :],
                        GBUF[:, r * 512:(r + 1) * 512],
                        start=True, stop=True, skip_group_check=True),
                        wait=[cld], inc=[ZMM],
                        reads=[f"gb{rb}"], writes=[f"pb{z}"])
                else:
                    tp.op("tensor", lambda h, r=rb, zz=z: mm_nold(
                        h, pb[zz][:, 0:512], CH[:, :],
                        GBUF[:, r * 512:(r + 1) * 512], True, True),
                        inc=[ZMM], reads=[f"gb{rb}"], writes=[f"pb{z}"])
            for rb in range(4):
                z = 4 + rb
                tp.op("vector", lambda h, r=rb, zz=z: h.tensor_copy(
                    HST[:, (r % 2) * 512:(r % 2) * 512 + 512], pb[zz][:, 0:512]),
                    inc=[ZEV], reads=[f"pb{z}"], writes=[f"hst{rb % 2}"])
                tp.op("scalar", lambda h, r=rb, t=t: h.dma_start(
                    s_hid[128 * t:128 * (t + 1), r * 512:(r + 1) * 512],
                    HST[:, (r % 2) * 512:(r % 2) * 512 + 512]),
                    inc=[HSW], dma=True, reads=[f"hst{rb % 2}"],
                    writes=["dma:hsw"])

        pending_z.append(_zbatch)
    flush_z()
    while bfly_q:
        bfly_q.pop(0)()

    tp.barrier("BC")
    hid_done = HSW.mark()
    w2h_done = BSW.mark()

    HQL, WCL, PJL, XBL, CMM, YEV, YSW = (
        S("HQL"), S("WCL"), S("PJL"), S("XBL"), S("CMM"), S("YEV"), S("YSW"))
    for q in range(4):
        srch = s_hid.rearrange("(ht p) r -> p ht r", p=128)[
            :, :, q * 512:(q + 1) * 512]
        dsth = A2[:, :].rearrange("p (ht r) -> p ht r", ht=64)
        tp.op("sync", lambda h, d=dsth, s=srch: h.dma_start(d, s),
              wait=[hid_done], inc=[HQL], dma=True, writes=["A2"])
        for dct in range(16):
            wb2 = dct % 2
            wcol = WB[:, :] if wb2 == 0 else XF[:, :]
            wcol_res = "wb_c" if wb2 == 0 else "xf_c"
            srcw = s_w2h[dct * 128:(dct + 1) * 128, :]
            tp.op("sync", lambda h, d=wcol, s=srcw: h.dma_start(d, s),
                  wait=[w2h_done], inc=[WCL], dma=True, writes=[wcol_res])
            srcp = wprojB[dct * 128:(dct + 1) * 128, :]
            dstp = XC[:, wb2 * 2048:(wb2 + 1) * 2048]
            tp.op("gpsimd", lambda h, d=dstp, s=srcp: h.dma_start(d, s),
                  inc=[PJL], dma=True, writes=[f"xc{wb2}"])
            tp.op("scalar", lambda h, d=dct, qi=q, w=wb2: h.dma_start(
                XBT[:, w * 512:(w + 1) * 512],
                xbt[128 * d:128 * (d + 1), qi * 512:(qi + 1) * 512]),
                inc=[XBL], dma=True, writes=[f"xbt{wb2}"])
            bko = dct % 2
            for ct in range(16):
                tp.op("tensor", lambda h, c=ct, w=wb2, qi=q, b=bko: h.matmul(
                    pb[b][:, 0:512],
                    XC[:, w * 2048 + c * 128: w * 2048 + (c + 1) * 128],
                    A1[:, c * 2048 + qi * 512: c * 2048 + qi * 512 + 512],
                    start=(c == 0), stop=False),
                    reads=([f"xc{wb2}", wcol_res, "A2"] if ct == 0 else []),
                    writes=([f"pb{bko}"] if ct == 0 else []))
            for C in range(64):
                tp.op("tensor", lambda h, c=C, w=wcol, b=bko: h.matmul(
                    pb[b][:, 0:512],
                    w[:, c * 128:(c + 1) * 128],
                    A2[:, c * 512:(c + 1) * 512],
                    start=False, stop=(c == 63)),
                    inc=([CMM] if C == 63 else []))
            cm2 = CMM.mark()
            tp.note_write(f"pb{bko}", cm2)
            tp.note_read(f"xc{wb2}", cm2)
            tp.note_read(wcol_res, cm2)
            tp.note_read("A2", cm2)
            tp.op("vector", lambda h, b=bko, w=wb2: h.tensor_add(
                YST[:, w * 512:(w + 1) * 512], pb[b][:, 0:512],
                XBT[:, w * 512:(w + 1) * 512]),
                inc=[YEV], reads=[f"pb{bko}", f"xbt{wb2}"], writes=[f"yst{wb2}"])
            tp.op("scalar", lambda h, d=dct, qi=q, w=wb2: h.dma_start(
                yT[128 * d:128 * (d + 1), qi * 512:(qi + 1) * 512],
                YST[:, w * 512:(w + 1) * 512]),
                inc=[YSW], dma=True, reads=[f"yst{wb2}"])
    tp.raw_wait("sync", YSW, YSW.n)

    with nc.Block() as block:
        tp.replay(block)
    ctx.close()
    return nc


_NC_CACHE = {}


def _get_program():
    if "nc" not in _NC_CACHE:
        _NC_CACHE["nc"] = build_program()
    return _NC_CACHE["nc"]


def _consts():
    H = np.array([[1.0]], np.float32)
    while H.shape[0] < 128:
        H = np.block([[H, H], [H, -H]])
    cid = np.eye(128, dtype=BF)
    cm = np.zeros((128, 128), dtype=BF)
    for t in range(8):
        cm[16 * t:16 * t + 16, 16 * t:16 * t + 16] = 1.0
    return cid, H.astype(BF), cm


def kernel(L0_attn_x, L0_attn_w_qkv, L0_attn_w_proj, L0_ffn_W1, L0_ffn_W2,
           L0_ffn_W3):
    X = np.ascontiguousarray(np.asarray(L0_attn_x, dtype=np.float32)[0])
    wqkvT = np.asarray(L0_attn_w_qkv, np.float32).T
    wprojT = np.asarray(L0_attn_w_proj, np.float32).T
    w1T = np.asarray(L0_ffn_W1, np.float32).T
    w3T = np.asarray(L0_ffn_W3, np.float32).T
    w2T = np.asarray(L0_ffn_W2, np.float32).T
    wqkvB = np.ascontiguousarray(
        wqkvT.reshape(16, 128, 24, 256).transpose(2, 1, 0, 3)).reshape(24 * 128, 4096)
    wprojB = np.ascontiguousarray(
        wprojT.reshape(16, 128, 16, 128).transpose(2, 1, 0, 3)).reshape(16 * 128, 2048)
    w1B = np.ascontiguousarray(
        w1T.reshape(16, 128, 64, 128).transpose(2, 1, 0, 3)).reshape(64 * 128, 2048)
    w3B = np.ascontiguousarray(
        w3T.reshape(16, 128, 64, 128).transpose(2, 1, 0, 3)).reshape(64 * 128, 2048)
    w2B = np.ascontiguousarray(
        w2T.reshape(64, 128, 16, 128).transpose(2, 1, 0, 3)).reshape(16 * 128, 8192)
    cid, ch, cm = _consts()
    cm4 = np.ascontiguousarray(np.tile(np.asarray(cm), (1, 4)))

    in_maps = []
    s_list = []
    for c in range(8):
        Sidx = np.array([h * 1024 + 128 * c + gg for h in range(16)
                         for gg in range(128)])
        s_list.append(Sidx)
        xbv = np.ascontiguousarray(X[Sidx])
        in_maps.append(dict(
            xa=np.ascontiguousarray(X[2048 * c:2048 * (c + 1)]),
            xb=xbv,
            xbt=np.ascontiguousarray(xbv.T),
            wqkvB=wqkvB, wprojB=wprojB, w1B=w1B, w3B=w3B, w2B=w2B,
            c_id=cid, c_h=ch, c_m=cm, c_m4=cm4,
        ))

    nc = _get_program()
    res = run_bass_kernel_spmd(nc, in_maps, core_ids=list(range(8)))
    out = np.empty((16384, 2048), np.float32)
    for c in range(8):
        out[s_list[c]] = res.results[c]["yT"].T
    return out.reshape(1, 16384, 2048)


# version 07
# speedup vs baseline: 1.6028x; 1.0051x over previous
import contextlib
import numpy as np
import ml_dtypes
import concourse.bass as bass
import concourse.mybir as mybir
from concourse.bass_utils import run_bass_kernel_spmd

F32 = mybir.dt.float32
BF16 = mybir.dt.bfloat16
AF = mybir.ActivationFunctionType
ALU = mybir.AluOpType
BF = ml_dtypes.bfloat16

ENGS = ("tensor", "vector", "scalar", "gpsimd", "sync")
D = 2048
NT = 2048
MQKV = 6144
HF = 8192
INV_SQRT_HD = float(1.0 / np.sqrt(128.0))
FWHT_SCALE = float(1.0 / np.sqrt(8192.0))
LN_EPS = 1e-5


class Sem:
    __slots__ = ("h", "n")

    def __init__(self, h):
        self.h = h
        self.n = 0

    def mark(self):
        return (self, self.n)


class Tape:

    def __init__(self, nc, ctx):
        self.nc = nc
        self.tape = {e: [] for e in ENGS}
        self._ctx = ctx
        self._sems = {}
        self.wr = {}
        self.rd = {}

    def sem(self, name):
        if name not in self._sems:
            self._sems[name] = Sem(self._ctx.enter_context(self.nc.semaphore(name)))
        return self._sems[name]

    def op(self, eng, fn, wait=(), inc=(), dma=False, reads=(), writes=()):
        waits = []
        for w in wait:
            if isinstance(w, Sem):
                waits.append((w, w.n))
            else:
                waits.append((w[0], w[1]))
        for r in reads:
            m = self.wr.get(r)
            if m is not None:
                waits.append(m)
        for r in writes:
            m = self.wr.get(r)
            if m is not None:
                waits.append(m)
            for mm in self.rd.get(r, ()):
                waits.append(mm)
        byname = {}
        for s, v in waits:
            if v > 0 and (s not in byname or v > byname[s]):
                byname[s] = v
        waits = [(s, v) for s, v in byname.items()]
        incs = []
        for i in inc:
            if isinstance(i, Sem):
                incs.append((i, 16 if dma else 1))
            else:
                incs.append((i[0], i[1]))
        for s, k in incs:
            s.n += k
        self.tape[eng].append((waits, fn, incs))
        mark = (incs[-1][0], incs[-1][0].n) if incs else None
        if mark is not None:
            for r in writes:
                self.wr[r] = mark
                self.rd[r] = []
            for r in reads:
                self.rd.setdefault(r, []).append(mark)

    def note_read(self, r, mark):
        self.rd.setdefault(r, []).append(mark)

    def note_write(self, r, mark):
        self.wr[r] = mark
        self.rd[r] = []

    def raw_wait(self, eng, sem, val):
        self.tape[eng].append(([(sem, val)], None, []))

    def barrier(self, name):
        b = self.sem("bar_" + name)
        target = b.n + 5
        for e in ENGS:
            self.tape[e].append(([], lambda h: h.nop(), [(b, 1)]))
        b.n = target
        for e in ENGS:
            self.tape[e].append(([(b, target)], None, []))
        self.wr = {k: v for k, v in self.wr.items() if k.startswith("dma:")}
        self.rd = {}

    def replay(self, block):
        def mk(eng_name):
            def body(h):
                for waits, fn, incs in self.tape[eng_name]:
                    for s, v in waits:
                        h.wait_ge(s.h, v)
                    if fn is None:
                        continue
                    instr = fn(h)
                    for s, k in incs:
                        instr.then_inc(s.h, k)
            return body

        block.tensor(mk("tensor"))
        block.vector(mk("vector"))
        block.scalar(mk("scalar"))
        block.gpsimd(mk("gpsimd"))
        block.sync(mk("sync"))


def build_program():
    nc = bass.Bass("TRN2", target_bir_lowering=False)

    xa = nc.dram_tensor("xa", [NT, D], F32, kind="ExternalInput").ap()
    xb = nc.dram_tensor("xb", [NT, D], F32, kind="ExternalInput").ap()
    xbt = nc.dram_tensor("xbt", [D, NT], F32, kind="ExternalInput").ap()
    wqkvB = nc.dram_tensor("wqkvB", [24 * 128, 4096], F32, kind="ExternalInput").ap()
    wprojB = nc.dram_tensor("wprojB", [16 * 128, 2048], F32, kind="ExternalInput").ap()
    w1B = nc.dram_tensor("w1B", [64 * 128, 2048], F32, kind="ExternalInput").ap()
    w3B = nc.dram_tensor("w3B", [64 * 128, 2048], F32, kind="ExternalInput").ap()
    w2B = nc.dram_tensor("w2B", [16 * 128, 8192], F32, kind="ExternalInput").ap()
    c_id = nc.dram_tensor("c_id", [128, 128], BF16, kind="ExternalInput").ap()
    c_h = nc.dram_tensor("c_h", [128, 128], BF16, kind="ExternalInput").ap()
    c_m = nc.dram_tensor("c_m", [128, 128], BF16, kind="ExternalInput").ap()
    c_m4 = nc.dram_tensor("c_m4", [128, 512], BF16, kind="ExternalInput").ap()
    yT = nc.dram_tensor("yT", [D, NT], F32, kind="ExternalOutput").ap()
    s_v = nc.dram_tensor("s_v", [NT, D], BF16, kind="Internal").ap()
    s_wq = nc.dram_tensor("s_wq", [24 * 128, 4096], BF16, kind="Internal").ap()
    s_hid = nc.dram_tensor("s_hid", [HF, NT], BF16, kind="Internal").ap()
    s_w2h = nc.dram_tensor("s_w2h", [D, HF], BF16, kind="Internal").ap()

    ctx = contextlib.ExitStack()
    sb = lambda name, shape, dt: ctx.enter_context(nc.sbuf_tensor(name, shape, dt))

    A1 = sb("A1", [128, 32768], BF16)
    A2 = sb("A2", [128, 32768], BF16)
    WB = sb("WB", [128, 8192], BF16)
    XF = sb("XF", [128, 8192], BF16)
    XC = sb("XC", [128, 4096], BF16)
    VINT = sb("VINT", [128, 4128], BF16)
    VST = sb("VST", [128, 512], BF16)
    SAB = sb("SAB", [128, 2048], BF16)
    SBB = sb("SBB", [128, 2048], BF16)
    GBUF = sb("GBUF", [128, 2048], BF16)
    HST = sb("HST", [128, 1024], BF16)
    YST = sb("YST", [128, 1024], F32)
    XBT = sb("XBT", [128, 1024], F32)
    EBUF = sb("EBUF", [128, 1024], BF16)
    OBUF = sb("OBUF", [128, 1024], BF16)
    RC = sb("RC", [128, 8], F32)
    SUMS = sb("SUMS", [128, 16], F32)
    SQS = sb("SQS", [128, 16], F32)
    MNEG = sb("MNEG", [128, 16], F32)
    RSTD = sb("RSTD", [128, 16], F32)
    TMPA = sb("TMPA", [128, 16], F32)
    TMPB = sb("TMPB", [128, 16], F32)
    CID = sb("CID", [128, 128], BF16)
    CH = sb("CH", [128, 128], BF16)
    CM = sb("CM", [128, 128], BF16)
    CM4 = sb("CM4", [128, 512], BF16)

    pb = [ctx.enter_context(nc.psum_tensor(f"pb{i}", [128, 512], F32)) for i in range(8)]
    DUMP = YST[:, :].bitcast(BF16)
    pbf = [p[:, :].bitcast(BF16) for p in pb]
    XFf32 = XF[:, :].bitcast(F32)

    tp = Tape(nc, ctx)
    S = tp.sem

    def mm_nold(h, out, lhsT, rhs, start, stop):
        inst = mybir.InstMatmult(
            name=nc.get_next_instruction_name(),
            ins=[h.lower_ap(rhs), h.lower_ap(lhsT, for_matmul_weights=True)],
            outs=[h.lower_ap(out)],
            replication_resolution=0, replication_shift_amnt=0,
            replication_num_rows=0,
            start_tensor_calc=start, stop_tensor_calc=stop,
            tile_size=(128, 128), tile_position=(0, 0), ldweights=False,
            bass_skip_group_check=True)
        return h.add_instruction(inst)
    CLD = S("CLD")

    for dst, src in ((CID, c_id), (CH, c_h), (CM, c_m), (CM4, c_m4)):
        tp.op("sync", lambda h, d=dst, s=src: h.dma_start(d[:, :], s), inc=[CLD], dma=True)
    cld = CLD.mark()
    ones_ap = bass.AP(VINT, 128, [[4128, 128], [129, 32], [1, 1]])
    tp.op("vector", lambda h: h.memset(ones_ap, 1.0), inc=[S("MISC")], writes=["vint_ones"])

    def emit_ln_tile(xsrc, tile, xn_col_base, P):
        b = tile % 2
        xld = XFf32[:, b * 2048:(b + 1) * 2048]
        xcb = XC[:, b * 2048:b * 2048 + 2048]
        col = tile % 16
        XL, XR, ST, VS, TT, TE = (S(P + "XL"), S(P + "XR"), S(P + "ST"),
                                  S(P + "VS"), S(P + "TT"), S(P + "TE"))
        tp.op("scalar", lambda h, x=xld, t=tile: h.dma_start(
            x, xsrc[128 * t:128 * (t + 1), :]), inc=[XL], dma=True,
            writes=[f"xf{b}"])
        tp.op("scalar", lambda h, x=xld, c=col: h.activation(
            DUMP[:, :], x, AF.Copy, bias=0.0, scale=np.float32(1.0),
            accum_out=SUMS[:, c:c + 1]), inc=[ST],
            reads=[f"xf{b}"], writes=[f"sum{col}"])
        tp.op("scalar", lambda h, x=xld, c=col: h.activation(
            DUMP[:, :], x, AF.Square, accum_out=SQS[:, c:c + 1]), inc=[ST],
            reads=[f"xf{b}"], writes=[f"sq{col}"])
        tp.op("vector", lambda h, c=col: h.tensor_scalar(
            MNEG[:, c:c + 1], SUMS[:, c:c + 1], float(-1.0 / 2048.0), None,
            ALU.mult), inc=[VS], reads=[f"sum{col}"], writes=[f"mneg{col}"])
        tp.op("vector", lambda h, c=col: h.tensor_scalar(
            TMPA[:, c:c + 1], SQS[:, c:c + 1], float(1.0 / 2048.0), None,
            ALU.mult), inc=[VS], reads=[f"sq{col}"], writes=[f"ta{col}"])
        tp.op("vector", lambda h, c=col: h.tensor_mul(
            TMPB[:, c:c + 1], MNEG[:, c:c + 1], MNEG[:, c:c + 1]), inc=[VS],
            reads=[f"mneg{col}"], writes=[f"tb{col}"])
        tp.op("vector", lambda h, c=col: h.tensor_sub(
            TMPA[:, c:c + 1], TMPA[:, c:c + 1], TMPB[:, c:c + 1]), inc=[VS],
            reads=[f"tb{col}"], writes=[f"ta{col}"])
        tp.op("scalar", lambda h, c=col: h.activation(
            TMPB[:, c:c + 1], TMPA[:, c:c + 1], AF.Ln, bias=np.float32(LN_EPS)),
            inc=[ST], reads=[f"ta{col}"], writes=[f"tb{col}"])
        tp.op("scalar", lambda h, c=col: h.activation(
            RSTD[:, c:c + 1], TMPB[:, c:c + 1], AF.Exp, scale=np.float32(-0.5)),
            inc=[ST], reads=[f"tb{col}"], writes=[f"rstd{col}"])
        tp.op("vector", lambda h, x=xld, xc=xcb, c=col: h.tensor_scalar(
            xc, x, MNEG[:, c:c + 1], RSTD[:, c:c + 1], ALU.add, ALU.mult),
            inc=[XR], reads=[f"xf{b}", f"mneg{col}", f"rstd{col}"],
            writes=[f"xc{b}"])
        for dt in range(16):
            bank = 6 + dt % 2
            tp.op("tensor", lambda h, xc=xcb, d=dt, bk=bank: h.transpose(
                pbf[bk][:, 0:128], xc[:, d * 128:(d + 1) * 128], CID[:, :]),
                wait=[cld], inc=[TT], reads=[f"xc{b}"], writes=[f"pb{bank}"])
            dcol = xn_col_base(dt, tile)
            tp.op("vector", lambda h, bk=bank, c=dcol: h.tensor_copy(
                A2[:, c:c + 128], pbf[bk][:, 0:128]), inc=[TE],
                reads=[f"pb{bank}"], writes=["A2lo" if P == "A" else "A2"])
        return TE.mark()

    WLD, VMM, VEV, SVW = S("WLD"), S("VMM"), S("VEV"), S("SVW")
    WQW = S("WQW")
    wq_cached = {}
    QKM, QEV, VGL = S("QKM"), S("QEV"), S("VGL")
    GQK, GEX, GMS, GAV, GRC, GTS, GTP, GPT = (
        S("GQK"), S("GEX"), S("GMS"), S("GAV"), S("GRC"), S("GTS"), S("GTP"), S("GPT"))

    ln_marks = []
    for half in range(2):
        for t8 in range(8):
            ln_marks.append(emit_ln_tile(xa, half * 8 + t8,
                            lambda dt, t: dt * 1024 + (t % 8) * 128, "A"))
        for mc in range(8):
            wbb = mc % 2
            blk = 16 + mc
            dst = WB[:, wbb * 4096:(wbb + 1) * 4096]
            if blk not in wq_cached:
                src = wqkvB[blk * 128:(blk + 1) * 128, :]
                tp.op("gpsimd", lambda h, d=dst, s=src: h.dma_start(d, s),
                      inc=[WLD], dma=True, writes=[f"wb{wbb}"])
                tp.op("sync", lambda h, d=s_wq[blk * 128:(blk + 1) * 128, :],
                      s=dst: h.dma_start(d, s),
                      inc=[WQW], dma=True, reads=[f"wb{wbb}"])
                wq_cached[blk] = WQW.mark()
            else:
                tp.op("sync", lambda h, d=dst,
                      s=s_wq[blk * 128:(blk + 1) * 128, :]: h.dma_start(d, s),
                      wait=[wq_cached[blk]],
                      inc=[WLD], dma=True, writes=[f"wb{wbb}"])
            for tt in range(8):
                bank = (mc * 8 + tt) % 2
                for dt in range(16):
                    tp.op("tensor", lambda h, b=bank, d=dt, t=tt, w=wbb: h.matmul(
                        pb[b][:, 0:256],
                        A2[:, d * 1024 + t * 128: d * 1024 + (t + 1) * 128],
                        WB[:, w * 4096 + d * 256: w * 4096 + d * 256 + 256],
                        start=(d == 0), stop=(d == 15)),
                        wait=([ln_marks[half * 8 + tt]] if dt == 0 else []),
                        inc=([VMM] if dt == 15 else []),
                        reads=([f"wb{wbb}"] if dt == 0 else []),
                        writes=([f"pb{bank}"] if dt == 0 else []))
                vm = VMM.mark()
                tp.note_write(f"pb{bank}", vm)
                tp.note_read("A2lo", vm)
                tp.note_read(f"wb{wbb}", vm)
                tp.op("vector", lambda h, b=bank: h.tensor_copy(
                    VST[:, b * 256:(b + 1) * 256], pb[b][:, 0:256]),
                    inc=[VEV], reads=[f"pb{bank}"], writes=[f"vst{bank}"])
                tp.op("sync", lambda h, b=bank, t=tt, m=mc, hf=half: h.dma_start(
                    s_v[128 * (hf * 8 + t):128 * (hf * 8 + t + 1),
                        m * 256:(m + 1) * 256],
                    VST[:, b * 256:(b + 1) * 256]),
                    inc=[SVW], dma=True, reads=[f"vst{bank}"])
        v_done = SVW.mark()
        for qr in range(2):
            qq = half * 2 + qr
            for mc in range(16):
                wbb = mc % 2
                dst = WB[:, wbb * 4096:(wbb + 1) * 4096]
                if mc not in wq_cached:
                    src = wqkvB[mc * 128:(mc + 1) * 128, :]
                    tp.op("gpsimd", lambda h, d=dst, s=src: h.dma_start(d, s),
                          inc=[WLD], dma=True, writes=[f"wb{wbb}"])
                    tp.op("sync", lambda h, d=s_wq[mc * 128:(mc + 1) * 128, :],
                          s=dst: h.dma_start(d, s),
                          inc=[WQW], dma=True, reads=[f"wb{wbb}"])
                    wq_cached[mc] = WQW.mark()
                else:
                    tp.op("sync", lambda h, d=dst,
                          s=s_wq[mc * 128:(mc + 1) * 128, :]: h.dma_start(d, s),
                          wait=[wq_cached[mc]],
                          inc=[WLD], dma=True, writes=[f"wb{wbb}"])
                for mt2 in range(2):
                    mt = 2 * mc + mt2
                    bank = mt % 4
                    for dt in range(16):
                        tp.op("tensor", lambda h, b=bank, d=dt, q=qr, w=wbb, m2=mt2: h.matmul(
                            pb[b][:, 0:512],
                            WB[:, w * 4096 + d * 256 + m2 * 128:
                               w * 4096 + d * 256 + m2 * 128 + 128],
                            A2[:, d * 1024 + q * 512: d * 1024 + (q + 1) * 512],
                            start=(d == 0), stop=(d == 15)),
                            wait=([ln_marks[half * 8 + 4 * qr + 3]] if dt == 0 else []),
                            inc=([QKM] if dt == 15 else []),
                            reads=([f"wb{wbb}"] if dt == 0 else []),
                            writes=([f"pb{bank}"] if dt == 0 else []))
                    qm = QKM.mark()
                    tp.note_write(f"pb{bank}", qm)
                    tp.note_read("A2lo", qm)
                    tp.note_read(f"wb{wbb}", qm)
                    i = mt % 16
                    base = 16384 + (8192 if mt >= 16 else 0) + i
                    dstq = bass.AP(A2, base, [[32768, 128], [128, 64], [16, 8]])
                    srcq = bass.AP(pb[bank], 0, [[512, 128], [8, 64], [1, 8]])
                    tp.op("vector", lambda h, d=dstq, s=srcq: h.tensor_copy(d, s),
                          inc=[QEV], reads=[f"pb{bank}"],
                          writes=["rQI" if mt < 16 else "rKI"])
            qk_done = QEV.mark()
            for tt4 in range(4):
                TTI = 4 * qq + tt4
                vb = tt4 % 2
                for tau in range(8):
                    dstv = bass.AP(VINT, tau * 16 * 4128 + vb * 2064,
                                   [[4128, 16], [129, 16], [1, 128]])
                    srcv = bass.AP(s_v.tensor, (128 * TTI + tau) * 2048,
                                   [[128, 16], [8 * 2048, 16], [1, 128]])
                    tp.op("sync", lambda h, d=dstv, s=srcv: h.dma_start(d, s),
                          wait=([v_done] if tau == 0 else []), inc=[VGL], dma=True,
                          reads=(["vint_ones"] if tau == 0 else []),
                          writes=([f"vint{vb}"] if tau == 0 else []),
                          )
                vload = VGL.mark()
                tp.note_write(f"vint{vb}", vload)
                for sg in range(4):
                    pE = sg % 2
                    for g in range(4):
                        gq = tt4 * 16 + sg * 4 + g
                        tp.op("tensor", lambda h, gq=gq, b=pE, g=g: h.matmul(
                            pb[b][:, g * 128:(g + 1) * 128],
                            A2[:, 24576 + gq * 128: 24576 + (gq + 1) * 128],
                            A2[:, 16384 + gq * 128: 16384 + (gq + 1) * 128],
                            start=True, stop=True, skip_group_check=True),
                            wait=[qk_done],
                            inc=([GQK] if g == 3 else []),
                            reads=(["rQI", "rKI"] if g == 0 else []),
                            writes=([f"pbE{pE}"] if g == 0 else []))
                    qkm = GQK.mark()
                    tp.note_write(f"pbE{pE}", qkm)
                    tp.note_read("rQI", qkm)
                    tp.note_read("rKI", qkm)
                    tp.op("scalar", lambda h, b=pE: h.activation(
                        EBUF[:, b * 512:(b + 1) * 512], pb[b][:, 0:512], AF.Exp,
                        bias=np.float32(-8.0), scale=np.float32(INV_SQRT_HD)),
                        inc=[GEX], reads=[f"pbE{pE}"], writes=[f"eb{pE}"])
                    tp.op("vector", lambda h, b=pE: h.tensor_mul(
                        EBUF[:, b * 512:(b + 1) * 512],
                        EBUF[:, b * 512:(b + 1) * 512], CM4[:, :]),
                        wait=[cld], inc=[GMS], reads=[f"eb{pE}"], writes=[f"eb{pE}"])
                    for g in range(4):
                        gl = tt4 * 16 + sg * 4 + g
                        elh = EBUF[:, pE * 512 + g * 128: pE * 512 + (g + 1) * 128]
                        tp.op("tensor", lambda h, e=elh, g=g, gl=gl, v=vb, b=pE:
                              h.matmul(
                                  pb[2 + b][:, g * 128:(g + 1) * 128], e,
                                  VINT[:, v * 2064 + (gl % 16) * 129:
                                       v * 2064 + (gl % 16) * 129 + 128],
                                  start=True, stop=True, skip_group_check=True),
                              wait=[vload],
                              reads=([f"eb{pE}", f"vint{vb}"] if g == 0 else []),
                              writes=([f"pbO{pE}"] if g == 0 else []))
                        tp.op("tensor", lambda h, e=elh, g=g, gl=gl, v=vb, b=pE:
                              mm_nold(h,
                                      pb[4 + b][:, 256 + g:256 + g + 1], e,
                                      VINT[:, v * 2064 + (gl % 16) * 129 + 128:
                                           v * 2064 + (gl % 16) * 129 + 129],
                                      True, True),
                              inc=([GAV] if g == 3 else []),
                              writes=([f"pbT{pE}"] if g == 0 else []))
                    avm = GAV.mark()
                    tp.note_write(f"pbO{pE}", avm)
                    tp.note_write(f"pbT{pE}", avm)
                    tp.note_read(f"eb{pE}", avm)
                    tp.note_read(f"vint{vb}", avm)
                    tp.op("vector", lambda h, b=pE: h.reciprocal(
                        RC[:, b * 4:(b + 1) * 4], pb[4 + b][:, 256:260]),
                        inc=[GRC], reads=[f"pbT{pE}"], writes=[f"rc{pE}"])
                    for g in range(4):
                        tp.op("vector", lambda h, b=pE, g=g: h.tensor_scalar(
                            OBUF[:, b * 512 + g * 128: b * 512 + (g + 1) * 128],
                            pb[2 + b][:, g * 128:(g + 1) * 128],
                            RC[:, b * 4 + g:b * 4 + g + 1], None, ALU.mult),
                            inc=([GTS] if g == 3 else []),
                            reads=([f"pbO{pE}", f"rc{pE}"] if g == 0 else []),
                            writes=([f"ob{pE}"] if g == 0 else []))
                    tsm = GTS.mark()
                    tp.note_write(f"ob{pE}", tsm)
                    tp.note_read(f"pbO{pE}", tsm)
                    tp.note_read(f"rc{pE}", tsm)
                    for g in range(4):
                        tp.op("tensor", lambda h, b=pE, g=g: h.transpose(
                            pbf[4 + b][:, g * 128:(g + 1) * 128],
                            OBUF[:, b * 512 + g * 128: b * 512 + (g + 1) * 128],
                            CID[:, :]),
                            inc=([GTP] if g == 3 else []),
                            reads=([f"ob{pE}"] if g == 0 else []),
                            writes=([f"pbT{pE}"] if g == 0 else []))
                    tpm = GTP.mark()
                    tp.note_write(f"pbT{pE}", tpm)
                    tp.note_read(f"ob{pE}", tpm)
                    for g in range(4):
                        gq = tt4 * 16 + sg * 4 + g
                        Gg = 64 * qq + gq
                        gg = Gg // 2
                        ctb = 8 * (Gg % 2)
                        dstp = bass.AP(A1, ctb * 2048 + gg,
                                       [[32768, 128], [2048, 8], [128, 16]])
                        srcp = pbf[4 + pE][:, g * 128:(g + 1) * 128].rearrange(
                            "p (t i) -> p t i", t=8)
                        tp.op("vector", lambda h, d=dstp, s=srcp: h.tensor_copy(d, s),
                              inc=([GPT] if g == 3 else []),
                              reads=([f"pbT{pE}"] if g == 0 else []))
                    ptm = GPT.mark()
                    tp.note_read(f"pbT{pE}", ptm)

    tp.barrier("AB")

    for tile in range(16):
        emit_ln_tile(xb, tile, lambda dt, t: dt * 2048 + t * 128, "B")
    tp.barrier("B0")

    BMA, BMB, SIL, ULC, GTM, ZMM, ZEV, HSW = (
        S("BMA"), S("BMB"), S("SIL"), S("ULC"), S("GTM"), S("ZMM"), S("ZEV"),
        S("HSW"))
    BLD, BFY, BSW = S("BLD"), S("BFY"), S("BSW")

    bfly_q = []

    def queue_butterfly_chunk(bc):
        srcl = w2B[bc * 128:(bc + 1) * 128, :]
        bfly_q.append(lambda s=srcl: tp.op(
            "gpsimd", lambda h, d=XF[:, :], ss=s: h.dma_start(d, ss),
            inc=[BLD], dma=True, writes=["bfA"], reads=["dma:bsw"]))
        for st in range(6):
            s2 = 1 << st
            na = 32 // s2
            inner = s2 * 128
            a_lo = bass.AP(XF, 0, [[8192, 128], [2 * inner, na], [1, inner]])
            a_hi = bass.AP(XF, inner, [[8192, 128], [2 * inner, na], [1, inner]])
            xct = bass.AP(XC, 0, [[4096, 128], [inner, na], [1, inner]])
            bfly_q.append(lambda o=xct, a=a_lo: tp.op(
                "vector", lambda h: h.tensor_copy(o, a),
                inc=[BFY], reads=["bfA"], writes=["xctmp"]))
            bfly_q.append(lambda o=a_lo, a=a_lo, b=a_hi: tp.op(
                "vector", lambda h: h.tensor_add(o, a, b),
                inc=[BFY], reads=["bfA"], writes=["bfA"]))
            bfly_q.append(lambda o=a_hi, a=xct, b=a_hi: tp.op(
                "vector", lambda h: h.tensor_sub(o, a, b),
                inc=[BFY], reads=["xctmp", "bfA"], writes=["bfA"]))
        dsts = s_w2h[bc * 128:(bc + 1) * 128, :]
        bfly_q.append(lambda d=dsts: tp.op(
            "sync", lambda h: h.dma_start(d, XF[:, :]),
            inc=[BSW], dma=True, reads=["bfA"], writes=["dma:bsw"]))

    for _bc in range(16):
        queue_butterfly_chunk(_bc)

    pending_z = []

    def flush_z():
        while pending_z:
            pending_z.pop(0)()

    for ht in range(64):
        hb = ht % 2
        for widx, wt in enumerate((w1B, w3B)):
            srcw = wt[ht * 128:(ht + 1) * 128, :]
            qoff = hb * 4096 + widx * 2048
            tp.op("gpsimd", lambda h, d=WB[:, qoff:qoff + 2048], s=srcw:
                  h.dma_start(d, s),
                  inc=[WLD], dma=True, writes=[f"wq{hb}{widx}"])
        wld_m = WLD.mark()
        flush_z()
        for _ in range(3):
            if bfly_q:
                bfly_q.pop(0)()
        for widx, msem in ((0, BMA), (1, BMB)):
            qoff = hb * 4096 + widx * 2048
            bb = 0 if widx == 0 else 4
            for dt in range(16):
                for rb in range(4):
                    bk = bb + rb
                    if rb == 0:
                        tp.op("tensor", lambda h, d=dt, qo=qoff, b=bk, r=rb:
                              h.matmul(
                                  pb[b][:, 0:512],
                                  WB[:, qo + d * 128: qo + (d + 1) * 128],
                                  A2[:, d * 2048 + r * 512:
                                     d * 2048 + (r + 1) * 512],
                                  start=(d == 0), stop=(d == 15),
                                  skip_group_check=True),
                              wait=([wld_m] if dt == 0 else []),
                              inc=([msem] if dt == 15 else []),
                              reads=(["A2", f"wq{hb}{widx}"] if dt == 0 else []),
                              writes=([f"pb{bk}"] if dt == 0 else []))
                    else:
                        tp.op("tensor", lambda h, d=dt, qo=qoff, b=bk, r=rb:
                              mm_nold(h, pb[b][:, 0:512],
                                      WB[:, qo + d * 128: qo + (d + 1) * 128],
                                      A2[:, d * 2048 + r * 512:
                                         d * 2048 + (r + 1) * 512],
                                      d == 0, d == 15),
                              inc=([msem] if dt == 15 else []),
                              writes=([f"pb{bk}"] if dt == 0 else []))
            base = msem.n - 4
            for rb in range(4):
                tp.note_write(f"pb{bb + rb}", (msem, base + rb + 1))
            tp.note_read("A2", (msem, msem.n))
            tp.note_read(f"wq{hb}{widx}", (msem, msem.n))
            if widx == 0:
                for rb in range(4):
                    tp.op("scalar", lambda h, r=rb: h.activation(
                        SAB[:, r * 512:(r + 1) * 512], pb[r][:, 0:512],
                        AF.Silu), inc=[SIL], reads=[f"pb{rb}"],
                        writes=[f"sab{rb}"])
        for rb in range(4):
            tp.op("scalar", lambda h, r=rb: h.activation(
                SBB[:, r * 512:(r + 1) * 512], pb[4 + r][:, 0:512],
                AF.Copy, bias=0.0, scale=np.float32(FWHT_SCALE)),
                inc=[ULC], reads=[f"pb{4 + rb}"], writes=[f"sbb{rb}"])
        for _ in range(2):
            if bfly_q:
                bfly_q.pop(0)()
        for rb in range(4):
            tp.op("vector", lambda h, r=rb: h.tensor_mul(
                GBUF[:, r * 512:(r + 1) * 512],
                SAB[:, r * 512:(r + 1) * 512],
                SBB[:, r * 512:(r + 1) * 512]),
                inc=[GTM], reads=[f"sab{rb}", f"sbb{rb}"], writes=[f"gb{rb}"])

        def _zbatch(t=ht):
            for rb in range(4):
                z = 4 + rb
                if rb == 0:
                    tp.op("tensor", lambda h, r=rb, zz=z: h.matmul(
                        pb[zz][:, 0:512], CH[:, :],
                        GBUF[:, r * 512:(r + 1) * 512],
                        start=True, stop=True, skip_group_check=True),
                        wait=[cld], inc=[ZMM],
                        reads=[f"gb{rb}"], writes=[f"pb{z}"])
                else:
                    tp.op("tensor", lambda h, r=rb, zz=z: mm_nold(
                        h, pb[zz][:, 0:512], CH[:, :],
                        GBUF[:, r * 512:(r + 1) * 512], True, True),
                        inc=[ZMM], reads=[f"gb{rb}"], writes=[f"pb{z}"])
            for rb in range(4):
                z = 4 + rb
                tp.op("vector", lambda h, r=rb, zz=z: h.tensor_copy(
                    HST[:, (r % 2) * 512:(r % 2) * 512 + 512], pb[zz][:, 0:512]),
                    inc=[ZEV], reads=[f"pb{z}"], writes=[f"hst{rb % 2}"])
                tp.op("scalar", lambda h, r=rb, t=t: h.dma_start(
                    s_hid[128 * t:128 * (t + 1), r * 512:(r + 1) * 512],
                    HST[:, (r % 2) * 512:(r % 2) * 512 + 512]),
                    inc=[HSW], dma=True, reads=[f"hst{rb % 2}"],
                    writes=["dma:hsw"])

        pending_z.append(_zbatch)
    flush_z()
    while bfly_q:
        bfly_q.pop(0)()

    tp.barrier("BC")
    hid_done = HSW.mark()
    w2h_done = BSW.mark()

    HQL, WCL, PJL, XBL, CMM, YEV, YSW = (
        S("HQL"), S("WCL"), S("PJL"), S("XBL"), S("CMM"), S("YEV"), S("YSW"))
    for q in range(4):
        srch = s_hid.rearrange("(ht p) r -> p ht r", p=128)[
            :, :, q * 512:(q + 1) * 512]
        dsth = A2[:, :].rearrange("p (ht r) -> p ht r", ht=64)
        tp.op("sync", lambda h, d=dsth, s=srch: h.dma_start(d, s),
              wait=[hid_done], inc=[HQL], dma=True, writes=["A2"])
        for dct in range(16):
            wb2 = dct % 2
            wcol = WB[:, :] if wb2 == 0 else XF[:, :]
            wcol_res = "wb_c" if wb2 == 0 else "xf_c"
            srcw = s_w2h[dct * 128:(dct + 1) * 128, :]
            tp.op("sync", lambda h, d=wcol, s=srcw: h.dma_start(d, s),
                  wait=[w2h_done], inc=[WCL], dma=True, writes=[wcol_res])
            srcp = wprojB[dct * 128:(dct + 1) * 128, :]
            dstp = XC[:, wb2 * 2048:(wb2 + 1) * 2048]
            tp.op("gpsimd", lambda h, d=dstp, s=srcp: h.dma_start(d, s),
                  inc=[PJL], dma=True, writes=[f"xc{wb2}"])
            tp.op("scalar", lambda h, d=dct, qi=q, w=wb2: h.dma_start(
                XBT[:, w * 512:(w + 1) * 512],
                xbt[128 * d:128 * (d + 1), qi * 512:(qi + 1) * 512]),
                inc=[XBL], dma=True, writes=[f"xbt{wb2}"])
            bko = dct % 2
            for ct in range(16):
                tp.op("tensor", lambda h, c=ct, w=wb2, qi=q, b=bko: h.matmul(
                    pb[b][:, 0:512],
                    XC[:, w * 2048 + c * 128: w * 2048 + (c + 1) * 128],
                    A1[:, c * 2048 + qi * 512: c * 2048 + qi * 512 + 512],
                    start=(c == 0), stop=False),
                    reads=([f"xc{wb2}", wcol_res, "A2"] if ct == 0 else []),
                    writes=([f"pb{bko}"] if ct == 0 else []))
            for C in range(64):
                tp.op("tensor", lambda h, c=C, w=wcol, b=bko: h.matmul(
                    pb[b][:, 0:512],
                    w[:, c * 128:(c + 1) * 128],
                    A2[:, c * 512:(c + 1) * 512],
                    start=False, stop=(c == 63)),
                    inc=([CMM] if C == 63 else []))
            cm2 = CMM.mark()
            tp.note_write(f"pb{bko}", cm2)
            tp.note_read(f"xc{wb2}", cm2)
            tp.note_read(wcol_res, cm2)
            tp.note_read("A2", cm2)
            tp.op("vector", lambda h, b=bko, w=wb2: h.tensor_add(
                YST[:, w * 512:(w + 1) * 512], pb[b][:, 0:512],
                XBT[:, w * 512:(w + 1) * 512]),
                inc=[YEV], reads=[f"pb{bko}", f"xbt{wb2}"], writes=[f"yst{wb2}"])
            tp.op("scalar", lambda h, d=dct, qi=q, w=wb2: h.dma_start(
                yT[128 * d:128 * (d + 1), qi * 512:(qi + 1) * 512],
                YST[:, w * 512:(w + 1) * 512]),
                inc=[YSW], dma=True, reads=[f"yst{wb2}"])
    tp.raw_wait("sync", YSW, YSW.n)

    with nc.Block() as block:
        tp.replay(block)
    ctx.close()
    return nc


_NC_CACHE = {}


def _get_program():
    if "nc" not in _NC_CACHE:
        _NC_CACHE["nc"] = build_program()
    return _NC_CACHE["nc"]


def _consts():
    H = np.array([[1.0]], np.float32)
    while H.shape[0] < 128:
        H = np.block([[H, H], [H, -H]])
    cid = np.eye(128, dtype=BF)
    cm = np.zeros((128, 128), dtype=BF)
    for t in range(8):
        cm[16 * t:16 * t + 16, 16 * t:16 * t + 16] = 1.0
    return cid, H.astype(BF), cm


def kernel(L0_attn_x, L0_attn_w_qkv, L0_attn_w_proj, L0_ffn_W1, L0_ffn_W2,
           L0_ffn_W3):
    X = np.ascontiguousarray(np.asarray(L0_attn_x, dtype=np.float32)[0])
    wqkvT = np.asarray(L0_attn_w_qkv, np.float32).T
    wprojT = np.asarray(L0_attn_w_proj, np.float32).T
    w1T = np.asarray(L0_ffn_W1, np.float32).T
    w3T = np.asarray(L0_ffn_W3, np.float32).T
    w2T = np.asarray(L0_ffn_W2, np.float32).T
    wqkvB = np.ascontiguousarray(
        wqkvT.reshape(16, 128, 24, 256).transpose(2, 1, 0, 3)).reshape(24 * 128, 4096)
    wprojB = np.ascontiguousarray(
        wprojT.reshape(16, 128, 16, 128).transpose(2, 1, 0, 3)).reshape(16 * 128, 2048)
    w1B = np.ascontiguousarray(
        w1T.reshape(16, 128, 64, 128).transpose(2, 1, 0, 3)).reshape(64 * 128, 2048)
    w3B = np.ascontiguousarray(
        w3T.reshape(16, 128, 64, 128).transpose(2, 1, 0, 3)).reshape(64 * 128, 2048)
    w2B = np.ascontiguousarray(
        w2T.reshape(64, 128, 16, 128).transpose(2, 1, 0, 3)).reshape(16 * 128, 8192)
    cid, ch, cm = _consts()
    cm4 = np.ascontiguousarray(np.tile(np.asarray(cm), (1, 4)))

    in_maps = []
    s_list = []
    for c in range(8):
        Sidx = np.array([h * 1024 + 128 * c + gg for h in range(16)
                         for gg in range(128)])
        s_list.append(Sidx)
        xbv = np.ascontiguousarray(X[Sidx])
        in_maps.append(dict(
            xa=np.ascontiguousarray(X[2048 * c:2048 * (c + 1)]),
            xb=xbv,
            xbt=np.ascontiguousarray(xbv.T),
            wqkvB=wqkvB, wprojB=wprojB, w1B=w1B, w3B=w3B, w2B=w2B,
            c_id=cid, c_h=ch, c_m=cm, c_m4=cm4,
        ))

    nc = _get_program()
    res = run_bass_kernel_spmd(nc, in_maps, core_ids=list(range(8)))
    out = np.empty((16384, 2048), np.float32)
    for c in range(8):
        out[s_list[c]] = res.results[c]["yT"].T
    return out.reshape(1, 16384, 2048)
